# version 02
# baseline (speedup 1.0000x reference)
import contextlib
import math
import numpy as np
import ml_dtypes
import concourse.bass as bass
import concourse.mybir as mybir
from concourse.bass_utils import run_bass_kernel_spmd

F32 = mybir.dt.float32
BF16 = mybir.dt.bfloat16
I32 = mybir.dt.int32
U16 = mybir.dt.uint16
AF = mybir.ActivationFunctionType
ALU = mybir.AluOpType
AX = mybir.AxisListType

NCORES = 8
B, S, D = 2, 4096, 2048
STRIP = 512
TPC = 1024
NT = TPC // 128
H0 = 16
HD = 128
NSTRIPS = S // STRIP

ENGINES = ["sync", "vector", "scalar", "tensor", "gpsimd"]
SERIALIZE = False


def strip_list(core):
    b, j = divmod(core, 4)
    return [(b, j), (b, 7 - j)]


def shard_rows(x):
    out = []
    for c in range(NCORES):
        (b1, s1), (b2, s2) = strip_list(c)
        out.append(np.ascontiguousarray(np.concatenate(
            [x[b1, s1 * STRIP:(s1 + 1) * STRIP], x[b2, s2 * STRIP:(s2 + 1) * STRIP]], axis=0)))
    return out


def unshard_rows(parts, dtype=np.float32):
    x = np.empty((B, S, D), dtype)
    for c in range(NCORES):
        (b1, s1), (b2, s2) = strip_list(c)
        x[b1, s1 * STRIP:(s1 + 1) * STRIP] = parts[c][:STRIP]
        x[b2, s2 * STRIP:(s2 + 1) * STRIP] = parts[c][STRIP:]
    return x


class Emit:

    def __init__(self, nc):
        self.nc = nc
        self.ops = {e: [] for e in ENGINES}
        self.sems = {}
        self.counts = {}
        self._es = contextlib.ExitStack()

    def sem(self, name):
        self.sems[name] = self._es.enter_context(self.nc.semaphore(name))
        self.counts[name] = 0
        return name

    def op(self, eng, fn, inc=None, inc_by=1):
        if inc is not None:
            self.counts[inc] += inc_by
        self.ops[eng].append(("op", fn, inc, inc_by))

    def dma(self, eng, out, in_, inc, **kw):
        self.op(eng, lambda h, o=out, i=in_, kw=kw: h.dma_start(out=o, in_=i, **kw),
                inc=inc, inc_by=16)

    def wait(self, eng, sem, val=None):
        v = self.counts[sem] if val is None else val
        self.ops[eng].append(("wait", sem, v, 0))

    def now(self, sem):
        return self.counts[sem]

    def barrier(self, vP, sP, tP):
        v, s_, t = self.counts[vP], self.counts[sP], self.counts[tP]
        self.wait("vector", sP, s_); self.wait("vector", tP, t)
        self.wait("scalar", vP, v); self.wait("scalar", tP, t)
        self.wait("tensor", vP, v); self.wait("tensor", sP, s_)

    def emit_and_close(self):
        with self.nc.Block() as block:
            for eng in ENGINES:
                ops = self.ops[eng]
                if not ops:
                    continue

                def body(h, ops=ops):
                    for rec in ops:
                        if rec[0] == "wait":
                            h.wait_ge(self.sems[rec[1]], rec[2])
                        else:
                            _, fn, inc, by = rec
                            inst = fn(h)
                            if inc is not None:
                                inst.then_inc(self.sems[inc], by)
                getattr(block, eng)(body)
        self._es.close()



def emit_ln(em, x_tiles, xln_tiles, stats, sq_scratch, tiles, vP, sP,
            out_dtype=BF16, rms=False, eps=1e-5):
    n = len(tiles)
    c = 1.0 / D
    sums = lambda i: stats[:, i:i + 1]
    for k, i in enumerate(tiles):
        if not rms:
            em.op("vector", lambda h, i=i, k=k: h.tensor_reduce(
                out=stats[:, k:k + 1], in_=x_tiles[i][:, :], axis=AX.X, op=ALU.add))
        em.op("scalar", lambda h, i=i, k=k: h.activation(
            out=xln_tiles[i][:, :], in_=x_tiles[i][:, :], func=AF.Square,
            accum_out=stats[:, n + k:n + k + 1]),
            inc=(sP if k == n - 1 else None))
    sq_done = em.now(sP)
    em.op("vector", lambda h: h.drain())
    em.wait("vector", sP, sq_done)
    if rms:
        em.op("vector", lambda h: h.tensor_scalar(
            out=stats[:, 3 * n:4 * n], in0=stats[:, n:2 * n], scalar1=c, scalar2=eps,
            op0=ALU.mult, op1=ALU.add))
    else:
        em.op("vector", lambda h: h.tensor_scalar(
            out=stats[:, 2 * n:3 * n], in0=stats[:, 0:n], scalar1=c,
            scalar2=None, op0=ALU.mult))
        em.op("vector", lambda h: h.drain())
        em.op("vector", lambda h: h.tensor_tensor(
            out=stats[:, 4 * n:5 * n], in0=stats[:, 2 * n:3 * n],
            in1=stats[:, 2 * n:3 * n], op=ALU.mult))
        em.op("vector", lambda h: h.drain())
        em.op("vector", lambda h: h.scalar_tensor_tensor(
            out=stats[:, 3 * n:4 * n], in0=stats[:, n:2 * n], scalar=c,
            in1=stats[:, 4 * n:5 * n], op0=ALU.mult, op1=ALU.subtract))
        em.op("vector", lambda h: h.drain())
        em.op("vector", lambda h: h.tensor_scalar(
            out=stats[:, 3 * n:4 * n], in0=stats[:, 3 * n:4 * n], scalar1=eps,
            scalar2=None, op0=ALU.add))
    em.op("vector", lambda h: h.drain())
    em.op("vector", lambda h: h.reciprocal(stats[:, 4 * n:5 * n], stats[:, 3 * n:4 * n]),
          inc=vP)
    rcp_done = em.now(vP)
    em.wait("scalar", vP, rcp_done)
    em.op("scalar", lambda h: h.activation(
        out=stats[:, 4 * n:5 * n], in_=stats[:, 4 * n:5 * n], func=AF.Sqrt), inc=sP)
    rstd_done = em.now(sP)
    em.wait("vector", sP, rstd_done)
    for k, i in enumerate(tiles):
        last = (sP if False else (vP if k == n - 1 else None))
        if rms:
            em.op("vector", lambda h, i=i, k=k: h.tensor_scalar(
                out=xln_tiles[i][:, :], in0=x_tiles[i][:, :],
                scalar1=stats[:, 4 * n + k:4 * n + k + 1], scalar2=None, op0=ALU.mult),
                inc=last)
        else:
            em.op("vector", lambda h, i=i, k=k: h.tensor_scalar(
                out=xln_tiles[i][:, :], in0=x_tiles[i][:, :],
                scalar1=stats[:, 2 * n + k:2 * n + k + 1],
                scalar2=stats[:, 4 * n + k:4 * n + k + 1],
                op0=ALU.subtract, op1=ALU.mult), inc=last)
    return em.now(vP)



def build_launchA(debug=False, stage=None, stageB=None):
    nc = bass.Bass("TRN2", target_bir_lowering=False)
    x_in = nc.dram_tensor("x", [TPC, D], F32, kind="ExternalInput").ap()
    ident_in = nc.dram_tensor("ident_hbm", [128, 128], BF16, kind="ExternalInput").ap()
    y_out = nc.dram_tensor("y", [TPC, D], F32, kind="ExternalOutput").ap()
    if debug:
        dbg_xfm = nc.dram_tensor("dbg_xfm", [128, H0 * STRIP], BF16, kind="ExternalOutput").ap()
        dbg_yfm = nc.dram_tensor("dbg_yfm", [128, H0 * STRIP], BF16, kind="ExternalOutput").ap()
        dbg_xln = nc.dram_tensor("dbg_xln", [128, D], BF16, kind="ExternalOutput").ap()
        dbg_sn1 = nc.dram_tensor("dbg_sn1", [128, H0 * STRIP], BF16, kind="ExternalOutput").ap()
        dbg_sn2 = nc.dram_tensor("dbg_sn2", [128, H0 * STRIP], BF16, kind="ExternalOutput").ap()
        dbg_id0 = nc.dram_tensor("dbg_id0", [128, 128], BF16, kind="ExternalOutput").ap()
        dbg_id1 = nc.dram_tensor("dbg_id1", [128, 128], BF16, kind="ExternalOutput").ap()

    es = contextlib.ExitStack()
    sb = lambda nm, shp, dt: es.enter_context(nc.sbuf_tensor(nm, shp, dt))
    ps = lambda nm, shp, dt: es.enter_context(nc.psum_tensor(nm, shp, dt))

    x_tiles = [sb(f"x{i}", [128, D], F32) for i in range(NT)]
    xln = [sb(f"xln{i}", [128, D], BF16) for i in range(NT)]
    stats = sb("stats", [128, 64], F32)
    sqs = sb("sqs", [128, D], BF16)
    ident = sb("ident", [128, 128], BF16)
    x_fm = sb("x_fm", [128, H0 * STRIP], BF16)
    snap1 = sb("snap1", [128, H0 * STRIP], BF16) if debug else None
    snap2 = sb("snap2", [128, H0 * STRIP], BF16) if debug else None
    idsnap = sb("idsnap", [128, 128], BF16) if debug else None
    y_fm = sb("y_fm", [128, H0 * STRIP], BF16)
    g_cmp = sb("g_cmp", [128, 256], BF16)
    gt = sb("gt", [128, 256], BF16)
    p_bf = sb("p_bf", [128, 512], BF16)
    pT = sb("pT", [128, 512], BF16)
    smax = sb("smax", [128, 8], F32)

    pt_a = ps("pt_a", [128, 1024], BF16)
    pt_b = ps("pt_b", [128, 1024], BF16)
    p_sc = ps("p_sc", [128, 512], F32)
    p_gt = ps("p_gt", [128, 1024], BF16)
    p_pT = ps("p_pT", [128, 1024], BF16)
    p_o = ps("p_o", [128, 512], F32)

    em = Emit(nc)
    ld = em.sem("ld")
    vP = em.sem("vP")
    sP = em.sem("sP")
    tP = em.sem("tP")
    st = em.sem("st")

    guard = {}

    def pe_write(bank):
        if bank in guard:
            sem, val = guard[bank]
            em.wait("tensor", sem, val)

    def consumed(bank, sem):
        guard[bank] = (sem, em.now(sem))

    for i in range(NT):
        em.dma("sync", x_tiles[i][:, :], x_in[128 * i:128 * (i + 1), :], ld)
    em.dma("sync", ident[:, :], ident_in, ld)
    ld_all = em.now(ld)

    SC = HD ** -0.5

    nstrips = 2 if stage is None else 1
    nheads = H0 if stage is None else min(stage % 100, H0)
    do_back = stage is None or stage >= 100
    for s in range(nstrips):
        t0 = 4 * s
        em.wait("vector", ld, ld_all)
        em.wait("scalar", ld, ld_all)
        if s == 1 and stageB == 13:
            break
        if s == 1 and stageB in (11, 12):
            if stageB == 11:
                for k, i in enumerate(range(t0, t0 + 4)):
                    em.op("vector", lambda h, i=i, k=k: h.tensor_reduce(
                        out=stats[:, k:k + 1], in_=x_tiles[i][:, :], axis=AX.X, op=ALU.add))
            else:
                for k, i in enumerate(range(t0, t0 + 4)):
                    em.op("scalar", lambda h, i=i, k=k: h.activation(
                        out=sqs[:, :], in_=x_tiles[i][:, :], func=AF.Square,
                        accum_out=stats[:, 4 + k:5 + k]))
            break
        ln_done = emit_ln(em, x_tiles, xln, stats, sqs, list(range(t0, t0 + 4)), vP, sP)
        if SERIALIZE:
            em.barrier(vP, sP, tP)
        if s == 1 and stageB is not None and stageB < 2:
            break
        em.wait("tensor", ld, ld_all)
        em.wait("tensor", vP, ln_done)
        em.op("vector", lambda h: h.memset(y_fm[:, :], 0.0), inc=vP)
        for t in range(4):
            for grp in range(4):
                pt = pt_a if grp % 2 == 0 else pt_b
                key = "pt_a" if grp % 2 == 0 else "pt_b"
                pe_write(key)
                for hh in range(4):
                    h_ = grp * 4 + hh
                    em.op("tensor", lambda h, t=t, h_=h_, pt=pt, hh=hh, t0=t0: h.transpose(
                        pt[:, 128 * hh:128 * (hh + 1)],
                        xln[t0 + t][:, 128 * h_:128 * (h_ + 1)], ident[:, :]),
                        inc=(tP if hh == 3 else None))
                em.wait("vector", tP, em.now(tP))
                for hh in range(4):
                    h_ = grp * 4 + hh
                    em.op("vector", lambda h, t=t, h_=h_, pt=pt, hh=hh: h.tensor_copy(
                        x_fm[:, 512 * h_ + 128 * t:512 * h_ + 128 * (t + 1)],
                        pt[:, 128 * hh:128 * (hh + 1)]),
                        inc=(vP if hh == 3 else None))
                consumed(key, vP)
        em.op("vector", lambda h: h.drain())
        if debug and s == 0:
            em.op("vector", lambda h: h.tensor_copy(snap1[:, :], x_fm[:, :]), inc=vP)
            em.op("vector", lambda h: h.tensor_copy(idsnap[:, :], ident[:, :]), inc=vP)
        fm_ready = em.now(vP)
        em.wait("tensor", vP, fm_ready)
        if SERIALIZE:
            em.barrier(vP, sP, tP)
        if s == 1 and stageB is not None and stageB < 3:
            break

        for hh in range(nheads):
            par = hh % 2
            em.op("vector", lambda h, hh=hh, par=par: h.tensor_copy(
                g_cmp[:, :], x_fm[:, 512 * hh + par:512 * (hh + 1):2]), inc=vP)
            em.wait("tensor", vP, em.now(vP))
            for kc in range(2):
                pe_write("p_gt")
                em.op("tensor", lambda h, kc=kc: h.transpose(
                    p_gt[:, 0:128], g_cmp[:, 128 * kc:128 * (kc + 1)], ident[:, :]),
                    inc=tP)
                em.wait("vector", tP, em.now(tP))
                em.op("vector", lambda h, kc=kc: h.tensor_copy(
                    gt[:, 128 * kc:128 * (kc + 1)], p_gt[:, 0:128]), inc=vP)
                consumed("p_gt", vP)
            for qc in range(2):
                pe_write("p_sc")
                em.op("tensor", lambda h, qc=qc: h.matmul(
                    p_sc[:, 0:256], g_cmp[:, 128 * qc:128 * (qc + 1)], g_cmp[:, :],
                    start=True, stop=True), inc=tP)
                em.wait("vector", tP, em.now(tP))
                em.op("vector", lambda h, qc=qc: h.tensor_reduce(
                    out=smax[:, qc:qc + 1], in_=p_sc[:, 0:256], axis=AX.X, op=ALU.max))
                em.op("vector", lambda h: h.drain())
                em.op("vector", lambda h, qc=qc: h.tensor_scalar(
                    out=smax[:, 2 + qc:3 + qc], in0=smax[:, qc:qc + 1],
                    scalar1=-SC, scalar2=None, op0=ALU.mult), inc=vP)
                em.wait("scalar", vP, em.now(vP))
                em.op("scalar", lambda h, qc=qc: h.activation(
                    out=p_bf[:, 256 * qc:256 * (qc + 1)], in_=p_sc[:, 0:256], func=AF.Exp,
                    bias=smax[:, 2 + qc:3 + qc], scale=SC,
                    accum_out=smax[:, 4 + qc:5 + qc]), inc=sP)
                consumed("p_sc", sP)
                em.wait("vector", sP, em.now(sP))
                em.op("vector", lambda h, qc=qc: h.reciprocal(
                    smax[:, 6 + qc:7 + qc], smax[:, 4 + qc:5 + qc]))
                em.op("vector", lambda h: h.drain())
                em.op("vector", lambda h, qc=qc: h.tensor_scalar(
                    out=p_bf[:, 256 * qc:256 * (qc + 1)],
                    in0=p_bf[:, 256 * qc:256 * (qc + 1)],
                    scalar1=smax[:, 6 + qc:7 + qc], scalar2=None,
                    op0=ALU.mult), inc=vP)
                em.wait("tensor", vP, em.now(vP))
                for kc in range(2):
                    pe_write("p_pT")
                    em.op("tensor", lambda h, qc=qc, kc=kc: h.transpose(
                        p_pT[:, 0:128],
                        p_bf[:, 256 * qc + 128 * kc:256 * qc + 128 * (kc + 1)],
                        ident[:, :]), inc=tP)
                    em.wait("vector", tP, em.now(tP))
                    em.op("vector", lambda h, qc=qc, kc=kc: h.tensor_copy(
                        pT[:, 256 * kc + 128 * qc:256 * kc + 128 * (qc + 1)],
                        p_pT[:, 0:128]), inc=vP)
                    consumed("p_pT", vP)
            em.wait("tensor", vP, em.now(vP))
            pe_write("p_o")
            for kc in range(2):
                em.op("tensor", lambda h, kc=kc: h.matmul(
                    p_o[:, 0:256], gt[:, 128 * kc:128 * (kc + 1)],
                    pT[:, 256 * kc:256 * (kc + 1)],
                    start=(kc == 0), stop=(kc == 1)),
                    inc=(tP if kc == 1 else None))
            em.wait("vector", tP, em.now(tP))
            em.op("vector", lambda h, hh=hh, par=par: h.tensor_copy(
                y_fm[:, 512 * hh + par:512 * (hh + 1):2], p_o[:, 0:256]), inc=vP)
            consumed("p_o", vP)
            if SERIALIZE:
                em.barrier(vP, sP, tP)
        em.op("vector", lambda h: h.drain())
        if debug and s == 0:
            em.op("vector", lambda h: h.tensor_copy(snap2[:, :], y_fm[:, :]), inc=vP)
        heads_done = em.now(vP)
        em.wait("tensor", vP, heads_done)
        if SERIALIZE:
            em.barrier(vP, sP, tP)
        if s == 1 and stageB is not None and stageB < 4:
            break
        for t in range(4):
            for grp in range(4):
                pt = pt_a if grp % 2 == 0 else pt_b
                key = "pt_a" if grp % 2 == 0 else "pt_b"
                pe_write(key)
                for hh in range(4):
                    h_ = grp * 4 + hh
                    em.op("tensor", lambda h, t=t, h_=h_, pt=pt, hh=hh: h.transpose(
                        pt[:, 128 * hh:128 * (hh + 1)],
                        y_fm[:, 512 * h_ + 128 * t:512 * h_ + 128 * (t + 1)],
                        ident[:, :]), inc=(tP if hh == 3 else None))
                em.wait("vector", tP, em.now(tP))
                em.op("vector", lambda h, t=t, grp=grp, pt=pt, t0=t0: h.tensor_tensor(
                    out=x_tiles[t0 + t][:, 512 * grp:512 * (grp + 1)],
                    in0=x_tiles[t0 + t][:, 512 * grp:512 * (grp + 1)],
                    in1=pt[:, 0:512], op=ALU.add), inc=vP)
                consumed(key, vP)
        if not do_back:
            break
        add_done = em.now(vP)
        em.wait("sync", vP, add_done)
        for t in range(4):
            em.dma("sync", y_out[128 * (t0 + t):128 * (t0 + t + 1), :],
                   x_tiles[t0 + t][:, :], st)
    if debug:
        em.wait("sync", vP, em.now(vP))
        em.wait("sync", sP, em.now(sP))
        em.wait("sync", tP, em.now(tP))
        em.dma("sync", dbg_xfm, x_fm[:, :], st)
        em.dma("sync", dbg_yfm, y_fm[:, :], st)
        em.dma("sync", dbg_xln, xln[0][:, :], st)
        em.dma("sync", dbg_sn1, snap1[:, :], st)
        em.dma("sync", dbg_sn2, snap2[:, :], st)
        em.dma("sync", dbg_id0, idsnap[:, :], st)
        em.dma("sync", dbg_id1, ident[:, :], st)
        if stage is not None and not do_back:
            for t in range(4):
                em.dma("sync", y_out[128 * t:128 * (t + 1), :], x_tiles[t][:, :], st)
            for t in range(4, NT):
                em.dma("sync", y_out[128 * t:128 * (t + 1), :], x_tiles[t][:, :], st)
    em.wait("sync", st, em.now(st))
    em.emit_and_close()
    es.close()
    return nc


IDENT_BF16 = np.eye(128, dtype=ml_dtypes.bfloat16)


def run_launchA(x_full, debug=False, stage=None, stageB=None):
    shards = shard_rows(np.asarray(x_full, np.float32))
    nc = build_launchA(debug=debug, stage=stage, stageB=stageB)
    in_maps = [{"x": s, "ident_hbm": IDENT_BF16} for s in shards]
    res = run_bass_kernel_spmd(nc, in_maps, core_ids=list(range(NCORES)))
    if debug:
        return unshard_rows([r["y"] for r in res.results]), res
    return unshard_rows([r["y"] for r in res.results])


def kernel(**inputs):
    y1 = run_launchA(inputs["L0_attn_x"])
    gu_all, d_all, sgu, sd = run_launchB1(inputs)
    y2 = run_launchB2(y1, inputs, gu_all, d_all, sgu, sd)
    q_list, k_list, v_list = run_launchC(y2, inputs)
    y3 = run_launchD(y2, inputs, q_list, k_list, v_list)
    qgT, quT, qdT = run_launchE12(inputs)
    y4 = run_launchE3(y3, qgT, quT, qdT)
    return y4



def kappa_orig_rows():
    out = np.empty(2048, np.int64)
    for c in range(2):
        for j in range(8):
            for p in range(128):
                out[1024 * c + 128 * j + p] = 1024 * c + 8 * p + j
    return out


def lam_orig_cols():
    out = np.empty(1536, np.int64)
    for j in range(8):
        for pg in range(192):
            out[192 * j + pg] = 8 * pg + j
    return out


KAPPA_ORIG = kappa_orig_rows()
LAM_ORIG = lam_orig_cols()
GU_COL_PERM = np.concatenate([LAM_ORIG, 1536 + LAM_ORIG])


def emit_dequant_job(em, bufs, w_src, sc_src, dst, W, N, n_chunks, dst_row_of,
                     sc_row_base_of, ld, lg, vP, st, tail_w=None):
    ALUx = ALU
    (wbuf, sc_t, plane, uu, u16t, mlo, v2, vfix, sfac, val, dense) = bufs
    for chunk in range(n_chunks):
        cw = W if (tail_w is None or chunk < n_chunks - 1) else tail_w
        em.dma("sync", wbuf[0:cw, :N], w_src[W * chunk:W * chunk + cw, :], ld)
        base = sc_row_base_of(chunk)
        ngr = (cw + 15) // 16
        src3 = sc_src[base:base + ngr][:, None, :].broadcast_to([ngr, 16, N])
        em.wait("gpsimd", vP, em.now(vP))
        em.dma("gpsimd", sc_t[0:cw, :N], src3, lg)
        em.wait("vector", ld, em.now(ld))
        em.wait("vector", lg, em.now(lg))
        for jp in range(4):
            for j in (2 * jp, 2 * jp + 1):
                b = j % 2
                if (chunk, jp) != (0, 0) or j != 2 * jp:
                    pass
            j0, j1 = 2 * jp, 2 * jp + 1
            em.wait("vector", st, em.now(st))
            for (dst_t, sh) in ((0, j0 * 4), (1, j1 * 4)):
                em.op("vector", lambda h, dst_t=dst_t, sh=sh, cw=cw: h.tensor_scalar(
                    out=plane[dst_t][0:cw, :N], in0=wbuf[0:cw, :N], scalar1=sh,
                    scalar2=15, op0=ALUx.logical_shift_right, op1=ALUx.bitwise_and)
                    if sh else h.tensor_scalar(
                    out=plane[dst_t][0:cw, :N], in0=wbuf[0:cw, :N], scalar1=15,
                    scalar2=None, op0=ALUx.bitwise_and))
            em.op("vector", lambda h: h.drain())
            for b in range(2):
                em.op("vector", lambda h, b=b, cw=cw: h.tensor_scalar(
                    out=uu[b][0:cw, :N], in0=plane[b][0:cw, :N], scalar1=7,
                    scalar2=None, op0=ALUx.bitwise_and))
            em.op("vector", lambda h: h.drain())
            for b in range(2):
                em.op("vector", lambda h, b=b, cw=cw: h.tensor_scalar(
                    out=u16t[b][0:cw, :N], in0=uu[b][0:cw, :N], scalar1=64,
                    scalar2=0x3F00, op0=ALUx.mult, op1=ALUx.add))
            em.op("vector", lambda h: h.drain())
            for b in range(2):
                em.op("vector", lambda h, b=b, cw=cw: h.tensor_scalar(
                    out=mlo[b][0:cw, :N], in0=uu[b][0:cw, :N], scalar1=2,
                    scalar2=None, op0=ALUx.is_lt))
            em.op("vector", lambda h: h.drain())
            for b in range(2):
                em.op("vector", lambda h, b=b, cw=cw: h.tensor_scalar(
                    out=v2[b][0:cw, :N], in0=u16t[b][0:cw, :N].bitcast(BF16),
                    scalar1=2.0, scalar2=1.0, op0=ALUx.mult, op1=ALUx.subtract))
            em.op("vector", lambda h: h.drain())
            for b in range(2):
                em.op("vector", lambda h, b=b, cw=cw: h.select(
                    vfix[b][0:cw, :N], mlo[b][0:cw, :N], v2[b][0:cw, :N],
                    u16t[b][0:cw, :N].bitcast(BF16), add_drain=True))
            em.op("vector", lambda h: h.drain())
            for b in range(2):
                em.op("vector", lambda h, b=b, cw=cw: h.tensor_scalar(
                    out=sfac[b][0:cw, :N], in0=plane[b][0:cw, :N], scalar1=8,
                    scalar2=None, op0=ALUx.is_ge))
            em.op("vector", lambda h: h.drain())
            for b in range(2):
                em.op("vector", lambda h, b=b, cw=cw: h.tensor_scalar(
                    out=sfac[b][0:cw, :N], in0=sfac[b][0:cw, :N], scalar1=-2.0,
                    scalar2=1.0, op0=ALUx.mult, op1=ALUx.add))
            em.op("vector", lambda h: h.drain())
            for b in range(2):
                em.op("vector", lambda h, b=b, cw=cw: h.tensor_tensor(
                    out=val[b][0:cw, :N], in0=sfac[b][0:cw, :N],
                    in1=vfix[b][0:cw, :N], op=ALUx.mult))
            em.op("vector", lambda h: h.drain())
            for b in range(2):
                em.op("vector", lambda h, b=b, cw=cw: h.tensor_tensor(
                    out=dense[b][0:cw, :N], in0=val[b][0:cw, :N],
                    in1=sc_t[0:cw, :N], op=ALUx.mult),
                    inc=(vP if b == 1 else None))
            em.op("vector", lambda h: h.drain())
            em.wait("sync", vP, em.now(vP))
            for b, j in ((0, j0), (1, j1)):
                r0 = dst_row_of(chunk, j)
                em.dma("sync", dst[r0:r0 + cw, :], dense[b][0:cw, :N], st)
        em.op("vector", lambda h: h.drain())
        em.wait("sync", vP, em.now(vP))


def build_launchB1():
    nc = bass.Bass("TRN2", target_bir_lowering=False)
    gu_pk = nc.dram_tensor("gu_pk", [2, 256, 3072], I32, kind="ExternalInput").ap()
    gu_sc = nc.dram_tensor("gu_sc", [2, 16, 3072], F32, kind="ExternalInput").ap()
    d_pk = nc.dram_tensor("d_pk", [2, 192, 2048], I32, kind="ExternalInput").ap()
    d_sc = nc.dram_tensor("d_sc", [2, 12, 2048], F32, kind="ExternalInput").ap()
    sgu_pk = nc.dram_tensor("sgu_pk", [256, 384], I32, kind="ExternalInput").ap()
    sgu_sc = nc.dram_tensor("sgu_sc", [16, 384], F32, kind="ExternalInput").ap()
    sd_pk = nc.dram_tensor("sd_pk", [192, 256], I32, kind="ExternalInput").ap()
    sd_sc = nc.dram_tensor("sd_sc", [12, 256], F32, kind="ExternalInput").ap()
    gu_out = nc.dram_tensor("gu_out", [2, 2048, 3072], BF16, kind="ExternalOutput").ap()
    d_out = nc.dram_tensor("d_out", [2, 1536, 2048], BF16, kind="ExternalOutput").ap()
    sgu_out = nc.dram_tensor("sgu_out", [2048, 384], BF16, kind="ExternalOutput").ap()
    sd_out = nc.dram_tensor("sd_out", [1536, 256], BF16, kind="ExternalOutput").ap()

    es = contextlib.ExitStack()
    sb = lambda nm, shp, dt: es.enter_context(nc.sbuf_tensor(nm, shp, dt))
    NMAX = 3072
    wbuf = sb("wbuf", [128, NMAX], I32)
    sc_t = sb("sc_t", [128, NMAX], BF16)
    plane = [sb(f"plane{b}", [128, NMAX], I32) for b in range(2)]
    uu = [sb(f"uu{b}", [128, NMAX], I32) for b in range(2)]
    u16t = [sb(f"u16t{b}", [128, NMAX], U16) for b in range(2)]
    mlo = [sb(f"mlo{b}", [128, NMAX], U16) for b in range(2)]
    v2 = [sb(f"v2{b}", [128, NMAX], BF16) for b in range(2)]
    vfix = [sb(f"vfix{b}", [128, NMAX], BF16) for b in range(2)]
    sfac = [sb(f"sfac{b}", [128, NMAX], BF16) for b in range(2)]
    val = [sb(f"val{b}", [128, NMAX], BF16) for b in range(2)]
    dense = [sb(f"dense{b}", [128, NMAX], BF16) for b in range(2)]
    bufs = (wbuf, sc_t, plane, uu, u16t, mlo, v2, vfix, sfac, val, dense)

    em = Emit(nc)
    ld = em.sem("ld")
    lg = em.sem("lg")
    vP = em.sem("vP")
    st = em.sem("st")

    for e in range(2):
        emit_dequant_job(em, bufs, gu_pk[e], gu_sc[e], gu_out[e], 128, 3072, 2,
                         lambda c, j: 1024 * c + 128 * j, lambda c: 8 * c,
                         ld, lg, vP, st)
        emit_dequant_job(em, bufs, d_pk[e], d_sc[e], d_out[e], 128, 2048, 2,
                         lambda c, j: 192 * j + 128 * c, lambda c: 8 * c,
                         ld, lg, vP, st, tail_w=64)
    emit_dequant_job(em, bufs, sgu_pk, sgu_sc, sgu_out, 128, 384, 2,
                     lambda c, j: 1024 * c + 128 * j, lambda c: 8 * c,
                     ld, lg, vP, st)
    emit_dequant_job(em, bufs, sd_pk, sd_sc, sd_out, 128, 256, 2,
                     lambda c, j: 192 * j + 128 * c, lambda c: 8 * c,
                     ld, lg, vP, st, tail_w=64)
    em.wait("sync", st, em.now(st))
    em.emit_and_close()
    es.close()
    return nc


def prep_moe_weight_inputs(inputs, core):
    gperm = GU_COL_PERM
    gu_pk = np.ascontiguousarray(inputs["L0_ffn_gate_up_packed"][2 * core:2 * core + 2][:, :, gperm])
    gu_sc = np.ascontiguousarray(inputs["L0_ffn_gate_up_scales"][2 * core:2 * core + 2][:, :, gperm])
    d_pk = np.ascontiguousarray(inputs["L0_ffn_down_packed"][2 * core:2 * core + 2])
    d_sc = np.ascontiguousarray(inputs["L0_ffn_down_scales"][2 * core:2 * core + 2])
    sgu_pk_all = inputs["L0_ffn_shared_gate_up_packed"][:, gperm]
    sgu_sc_all = inputs["L0_ffn_shared_gate_up_scales"][:, gperm]
    lo, hi = 192 * core, 192 * (core + 1)
    cols = np.r_[lo:hi, 1536 + lo:1536 + hi]
    sgu_pk = np.ascontiguousarray(sgu_pk_all[:, cols])
    sgu_sc = np.ascontiguousarray(sgu_sc_all[:, cols])
    sd_pk = np.ascontiguousarray(inputs["L0_ffn_shared_down_packed"][:, 256 * core:256 * (core + 1)])
    sd_sc = np.ascontiguousarray(inputs["L0_ffn_shared_down_scales"][:, 256 * core:256 * (core + 1)])
    return {"gu_pk": gu_pk, "gu_sc": gu_sc, "d_pk": d_pk, "d_sc": d_sc,
            "sgu_pk": sgu_pk, "sgu_sc": sgu_sc, "sd_pk": sd_pk, "sd_sc": sd_sc}


def run_launchB1(inputs):
    nc = build_launchB1()
    in_maps = [prep_moe_weight_inputs(inputs, c) for c in range(NCORES)]
    res = run_bass_kernel_spmd(nc, in_maps, core_ids=list(range(NCORES)))
    gu_all = np.stack([np.asarray(r["gu_out"]) for r in res.results]).reshape(16, 2048, 3072)
    d_all = np.stack([np.asarray(r["d_out"]) for r in res.results]).reshape(16, 1536, 2048)
    sgu_gate = np.concatenate([np.asarray(r["sgu_out"])[:, :192] for r in res.results], axis=1)
    sgu_up = np.concatenate([np.asarray(r["sgu_out"])[:, 192:] for r in res.results], axis=1)
    sgu = np.concatenate([sgu_gate, sgu_up], axis=1)
    sd = np.concatenate([np.asarray(r["sd_out"]) for r in res.results], axis=1)
    return gu_all, d_all, sgu, sd



def build_launchB2():
    nc = bass.Bass("TRN2", target_bir_lowering=False)
    y_in = nc.dram_tensor("y1", [TPC, D], F32, kind="ExternalInput").ap()
    gwT_in = nc.dram_tensor("gwT", [D, 16], F32, kind="ExternalInput").ap()
    ident_in = nc.dram_tensor("ident_hbm", [128, 128], BF16, kind="ExternalInput").ap()
    gu_all = nc.dram_tensor("gu_all", [16, 2048, 3072], BF16, kind="ExternalInput").ap()
    d_all = nc.dram_tensor("d_all", [16, 1536, 2048], BF16, kind="ExternalInput").ap()
    sgu = nc.dram_tensor("sgu", [2048, 3072], BF16, kind="ExternalInput").ap()
    sd = nc.dram_tensor("sd", [1536, 2048], BF16, kind="ExternalInput").ap()
    y_out = nc.dram_tensor("y2", [TPC, D], F32, kind="ExternalOutput").ap()

    es = contextlib.ExitStack()
    sb = lambda nm, shp, dt: es.enter_context(nc.sbuf_tensor(nm, shp, dt))
    ps = lambda nm, shp, dt: es.enter_context(nc.psum_tensor(nm, shp, dt))

    x_tiles = [sb(f"x{i}", [128, D], F32) for i in range(NT)]
    xln = [sb(f"xln{i}", [128, D], BF16) for i in range(NT)]
    stats = sb("stats", [128, 64], F32)
    ident = sb("ident", [128, 128], BF16)
    gw_f = sb("gw_f", [128, 16 * 16], F32)
    gw_b = sb("gw_b", [128, 16 * 16], BF16)
    x_fm = sb("x_fm", [128, 16 * TPC], BF16)
    comb = sb("comb", [128, NT * 16], F32)
    lg_sb = sb("lg_sb", [128, 16], F32)
    m8 = sb("m8", [128, 8], F32)
    gsc = sb("gsc", [128, 8], F32)
    eun = sb("eun", [128, 16], F32)
    msk = sb("msk", [128, 16], F32)
    pm = sb("pm", [128, 16], F32)
    gu_sb = sb("gu_sb", [128, 16 * 1536], BF16)
    dsb = sb("dsb", [128, 6 * 2048], BF16)
    g_act = sb("g_act", [128, 384], BF16)
    a_tm = sb("a_tm", [128, 768], BF16)
    a_fm = sb("a_fm", [128, 768], BF16)

    p_gate = ps("p_gate", [128, 512], F32)
    p_g = ps("p_g", [128, 512], F32)
    p_u = ps("p_u", [128, 512], F32)
    p_y = [ps(f"p_y{n}", [128, 512], F32) for n in range(4)]
    p_tr = ps("p_tr", [128, 1024], BF16)

    em = Emit(nc)
    ld = em.sem("ld")
    vP = em.sem("vP")
    sP = em.sem("sP")
    tP = em.sem("tP")
    st = em.sem("st")

    guard = {}

    def pe_write(bank):
        if bank in guard:
            s_, v_ = guard[bank]
            em.wait("tensor", s_, v_)

    def consumed(bank, sem):
        guard[bank] = (sem, em.now(sem))

    for i in range(NT):
        em.dma("sync", x_tiles[i][:, :], y_in[128 * i:128 * (i + 1), :], ld)
    em.dma("sync", ident[:, :], ident_in, ld)
    for k in range(16):
        em.dma("sync", gw_f[:, 16 * k:16 * (k + 1)], gwT_in[128 * k:128 * (k + 1), :], ld)
    ld_all = em.now(ld)
    em.wait("vector", ld, ld_all)
    em.wait("scalar", ld, ld_all)
    em.op("vector", lambda h: h.tensor_copy(gw_b[:, :], gw_f[:, :]), inc=vP)

    ln_done = emit_ln(em, x_tiles, xln, stats, None, list(range(NT)), vP, sP)

    em.wait("tensor", ld, ld_all)
    em.wait("tensor", vP, ln_done)
    for k in range(16):
        c, j = divmod(k, 8)
        for t in range(NT):
            pe_write("p_tr")
            em.op("tensor", lambda h, k=k, c=c, j=j, t=t: h.transpose(
                p_tr[:, 0:128], xln[t][:, 1024 * c + j:1024 * (c + 1):8],
                ident[:, :]), inc=tP)
            em.wait("vector", tP, em.now(tP))
            em.op("vector", lambda h, k=k, t=t: h.tensor_copy(
                x_fm[:, 1024 * k + 128 * t:1024 * k + 128 * (t + 1)], p_tr[:, 0:128]),
                inc=vP)
            consumed("p_tr", vP)
    fm_ready = em.now(vP)
    em.wait("tensor", vP, fm_ready)

    for t in range(NT):
        pe_write("p_gate")
        for k in range(16):
            em.op("tensor", lambda h, k=k, t=t: h.matmul(
                p_gate[:, 0:16], x_fm[:, 1024 * k + 128 * t:1024 * k + 128 * (t + 1)],
                gw_b[:, 16 * k:16 * (k + 1)], start=(k == 0), stop=(k == 15)),
                inc=(tP if k == 15 else None))
        em.wait("vector", tP, em.now(tP))
        em.op("vector", lambda h: h.tensor_copy(lg_sb[:, :], p_gate[:, 0:16]))
        em.op("vector", lambda h: h.drain())
        em.op("vector", lambda h: h.max(out=m8[:, :], in_=lg_sb[:, :]))
        em.op("vector", lambda h: h.drain())
        em.op("vector", lambda h: h.tensor_scalar(
            out=gsc[:, 0:1], in0=m8[:, 0:1], scalar1=-1.0, scalar2=None,
            op0=ALU.mult), inc=vP)
        em.wait("scalar", vP, em.now(vP))
        em.op("scalar", lambda h: h.activation(
            out=eun[:, :], in_=lg_sb[:, :], func=AF.Exp, bias=gsc[:, 0:1],
            scale=1.0), inc=sP)
        consumed("p_gate", sP)
        em.wait("vector", sP, em.now(sP))
        em.op("vector", lambda h: h.tensor_scalar(
            out=msk[:, :], in0=lg_sb[:, :], scalar1=m8[:, 3:4], scalar2=None,
            op0=ALU.is_ge))
        em.op("vector", lambda h: h.drain())
        em.op("vector", lambda h: h.tensor_tensor(
            out=pm[:, :], in0=eun[:, :], in1=msk[:, :], op=ALU.mult))
        em.op("vector", lambda h: h.drain())
        em.op("vector", lambda h: h.tensor_reduce(
            out=gsc[:, 1:2], in_=pm[:, :], axis=AX.X, op=ALU.add))
        em.op("vector", lambda h: h.drain())
        em.op("vector", lambda h: h.reciprocal(gsc[:, 2:3], gsc[:, 1:2]))
        em.op("vector", lambda h: h.drain())
        em.op("vector", lambda h, t=t: h.tensor_scalar(
            out=comb[:, 16 * t:16 * (t + 1)], in0=pm[:, :], scalar1=gsc[:, 2:3],
            scalar2=None, op0=ALU.mult), inc=vP)
        em.op("vector", lambda h: h.drain())
    gate_done = em.now(vP)

    em.wait("sync", ld, ld_all)
    for e in range(17):
        shared = (e == 16)
        for ih in range(2):
            em.wait("sync", tP, em.now(tP))
            for k in range(16):
                for sec in range(2):
                    srcc = (sgu if shared else gu_all[e])
                    em.dma("sync",
                           gu_sb[:, 1536 * k + 768 * sec:1536 * k + 768 * (sec + 1)],
                           srcc[128 * k:128 * (k + 1),
                                1536 * sec + 768 * ih:1536 * sec + 768 * (ih + 1)],
                           ld)
            for k2 in range(6):
                srcd = (sd if shared else d_all[e])
                em.dma("sync", dsb[:, 2048 * k2:2048 * (k2 + 1)],
                       srcd[768 * ih + 128 * k2:768 * ih + 128 * (k2 + 1), :], ld)
            w_ready = em.now(ld)
            em.wait("tensor", ld, w_ready)
            for t in range(NT):
                for p in range(2):
                    cb = 384 * p
                    pe_write("p_g")
                    pe_write("p_u")
                    for k in range(16):
                        em.op("tensor", lambda h, k=k, t=t, cb=cb: h.matmul(
                            p_g[:, 0:384],
                            x_fm[:, 1024 * k + 128 * t:1024 * k + 128 * (t + 1)],
                            gu_sb[:, 1536 * k + cb:1536 * k + cb + 384],
                            start=(k == 0), stop=(k == 15)))
                        em.op("tensor", lambda h, k=k, t=t, cb=cb: h.matmul(
                            p_u[:, 0:384],
                            x_fm[:, 1024 * k + 128 * t:1024 * k + 128 * (t + 1)],
                            gu_sb[:, 1536 * k + 768 + cb:1536 * k + 768 + cb + 384],
                            start=(k == 0), stop=(k == 15)),
                            inc=(tP if k == 15 else None))
                    em.wait("scalar", tP, em.now(tP))
                    em.wait("scalar", vP, em.now(vP))
                    em.op("scalar", lambda h: h.activation(
                        out=g_act[:, :], in_=p_g[:, 0:384], func=AF.Silu), inc=sP)
                    consumed("p_g", sP)
                    em.wait("vector", sP, em.now(sP))
                    em.op("vector", lambda h, p=p: h.tensor_tensor(
                        out=a_tm[:, 384 * p:384 * (p + 1)], in0=p_u[:, 0:384],
                        in1=g_act[:, :], op=ALU.mult), inc=vP)
                    consumed("p_u", vP)
                a_done = em.now(vP)
                em.wait("tensor", vP, a_done)
                for k2 in range(6):
                    pe_write("p_tr")
                    em.op("tensor", lambda h, k2=k2: h.transpose(
                        p_tr[:, 0:128], a_tm[:, 128 * k2:128 * (k2 + 1)],
                        ident[:, :]), inc=tP)
                    em.wait("vector", tP, em.now(tP))
                    em.op("vector", lambda h, k2=k2: h.tensor_copy(
                        a_fm[:, 128 * k2:128 * (k2 + 1)], p_tr[:, 0:128]), inc=vP)
                    consumed("p_tr", vP)
                em.wait("tensor", vP, em.now(vP))
                for n in range(4):
                    pe_write(f"p_y{n}")
                for k2 in range(6):
                    for n in range(4):
                        em.op("tensor", lambda h, k2=k2, n=n: h.matmul(
                            p_y[n][:, 0:512], a_fm[:, 128 * k2:128 * (k2 + 1)],
                            dsb[:, 2048 * k2 + 512 * n:2048 * k2 + 512 * (n + 1)],
                            start=(k2 == 0), stop=(k2 == 5)),
                            inc=(tP if (k2 == 5 and n == 3) else None))
                em.wait("vector", tP, em.now(tP))
                for n in range(4):
                    if shared:
                        em.op("vector", lambda h, t=t, n=n: h.tensor_tensor(
                            out=x_tiles[t][:, 512 * n:512 * (n + 1)],
                            in0=x_tiles[t][:, 512 * n:512 * (n + 1)],
                            in1=p_y[n][:, 0:512], op=ALU.add),
                            inc=(vP if n == 3 else None))
                    else:
                        em.op("vector", lambda h, t=t, n=n, e=e: h.scalar_tensor_tensor(
                            out=x_tiles[t][:, 512 * n:512 * (n + 1)],
                            in0=p_y[n][:, 0:512], scalar=comb[:, 16 * t + e:16 * t + e + 1],
                            in1=x_tiles[t][:, 512 * n:512 * (n + 1)],
                            op0=ALU.mult, op1=ALU.add),
                            inc=(vP if n == 3 else None))
                em.op("vector", lambda h: h.drain())
                for n in range(4):
                    consumed(f"p_y{n}", vP)
    done = em.now(vP)
    em.wait("sync", vP, done)
    for t in range(NT):
        em.dma("sync", y_out[128 * t:128 * (t + 1), :], x_tiles[t][:, :], st)
    em.wait("sync", st, em.now(st))
    em.emit_and_close()
    es.close()
    return nc


def gate_wT_perm(gate_w):
    return np.ascontiguousarray(gate_w.T[KAPPA_ORIG])


def run_launchB2(y1_full, inputs, gu_all, d_all, sgu, sd):
    shards = shard_rows(np.asarray(y1_full, np.float32))
    gwT = gate_wT_perm(np.asarray(inputs["L0_ffn_gate_w"], np.float32))
    nc = build_launchB2()
    in_maps = [{"y1": s, "gwT": gwT, "ident_hbm": IDENT_BF16,
                "gu_all": gu_all, "d_all": d_all, "sgu": sgu, "sd": sd}
               for s in shards]
    res = run_bass_kernel_spmd(nc, in_maps, core_ids=list(range(NCORES)))
    return unshard_rows([np.asarray(r["y2"]) for r in res.results])



def build_launchC():
    nc = bass.Bass("TRN2", target_bir_lowering=False)
    y_in = nc.dram_tensor("y2", [TPC, D], F32, kind="ExternalInput").ap()
    w_in = {}
    b_in = {}
    for w in ("q", "k", "v"):
        w_in[w] = nc.dram_tensor(f"W{w}", [D, D], F32, kind="ExternalInput").ap()
        b_in[w] = nc.dram_tensor(f"b{w}", [1, D], F32, kind="ExternalInput").ap()
    ident_in = nc.dram_tensor("ident_hbm", [128, 128], BF16, kind="ExternalInput").ap()
    q_out = nc.dram_tensor("q_fm", [D, TPC], BF16, kind="ExternalOutput").ap()
    k_out = nc.dram_tensor("k_fm", [D, TPC], BF16, kind="ExternalOutput").ap()
    v_out = nc.dram_tensor("v_tm", [TPC, D], BF16, kind="ExternalOutput").ap()

    es = contextlib.ExitStack()
    sb = lambda nm, shp, dt: es.enter_context(nc.sbuf_tensor(nm, shp, dt))
    ps = lambda nm, shp, dt: es.enter_context(nc.psum_tensor(nm, shp, dt))

    x_tiles = [sb(f"x{i}", [128, D], F32) for i in range(NT)]
    xln = [sb(f"xln{i}", [128, D], BF16) for i in range(NT)]
    stats = sb("stats", [128, 64], F32)
    ident = sb("ident", [128, 128], BF16)
    x_fm = sb("x_fm", [128, 16 * TPC], BF16)
    wsb = sb("wsb", [128, 16 * D], BF16)
    bias = sb("bias", [1, D], BF16)
    ones1 = sb("ones1", [1, 128], BF16)
    oneT = sb("oneT", [1, 512], BF16)
    outb = sb("outb", [128, D], BF16)
    o_fm = sb("o_fm", [128, 512], BF16)
    p_tr = ps("p_tr", [128, 1024], BF16)
    p_o = [ps(f"p_o{i}", [128, 512], F32) for i in range(2)]

    em = Emit(nc)
    ld = em.sem("ld")
    lg = em.sem("lg")
    vP = em.sem("vP")
    sP = em.sem("sP")
    tP = em.sem("tP")
    st = em.sem("st")

    guard = {}

    def pe_write(bank):
        if bank in guard:
            s_, v_ = guard[bank]
            em.wait("tensor", s_, v_)

    def consumed(bank, sem):
        guard[bank] = (sem, em.now(sem))

    for i in range(NT):
        em.dma("sync", x_tiles[i][:, :], y_in[128 * i:128 * (i + 1), :], ld)
    em.dma("sync", ident[:, :], ident_in, ld)
    ld_all = em.now(ld)
    em.wait("vector", ld, ld_all)
    em.wait("scalar", ld, ld_all)
    em.op("vector", lambda h: h.memset(ones1[:, :], 1.0))
    em.op("vector", lambda h: h.memset(oneT[:, :], 1.0), inc=vP)
    ln_done = emit_ln(em, x_tiles, xln, stats, None, list(range(NT)), vP, sP)
    em.wait("tensor", ld, ld_all)
    em.wait("tensor", vP, ln_done)
    for k in range(16):
        c, j = divmod(k, 8)
        for t in range(NT):
            pe_write("p_tr")
            em.op("tensor", lambda h, c=c, j=j, t=t: h.transpose(
                p_tr[:, 0:128], xln[t][:, 1024 * c + j:1024 * (c + 1):8],
                ident[:, :]), inc=tP)
            em.wait("vector", tP, em.now(tP))
            em.op("vector", lambda h, k=k, t=t: h.tensor_copy(
                x_fm[:, 1024 * k + 128 * t:1024 * k + 128 * (t + 1)], p_tr[:, 0:128]),
                inc=vP)
            consumed("p_tr", vP)
    fm_ready = em.now(vP)
    em.wait("tensor", vP, fm_ready)

    for wi, wname in enumerate(("q", "k", "v")):
        em.wait("gpsimd", tP, em.now(tP))
        for k in range(16):
            em.dma("gpsimd", wsb[:, D * k:D * (k + 1)],
                   w_in[wname][128 * k:128 * (k + 1), :], lg)
        em.dma("gpsimd", bias[:, :], b_in[wname], lg)
        w_ready = em.now(lg)
        em.wait("tensor", lg, w_ready)
        if wname in ("q", "k"):
            dst = q_out if wname == "q" else k_out
            for m in range(16):
                for th in range(2):
                    bank = th % 2
                    pe_write(f"p_o{bank}")
                    for k in range(16):
                        em.op("tensor", lambda h, m=m, th=th, k=k, bank=bank: h.matmul(
                            p_o[bank][:, :],
                            wsb[:, D * k + 128 * m:D * k + 128 * (m + 1)],
                            x_fm[:, 1024 * k + 512 * th:1024 * k + 512 * (th + 1)],
                            start=(k == 0), stop=False))
                    em.op("tensor", lambda h, m=m, th=th, bank=bank: h.matmul(
                        p_o[bank][:, :],
                        bias[:, 128 * m:128 * (m + 1)], oneT[:, :],
                        start=False, stop=True), inc=tP)
                    em.wait("vector", tP, em.now(tP))
                    em.op("vector", lambda h, bank=bank: h.tensor_copy(
                        o_fm[:, :], p_o[bank][:, :]), inc=vP)
                    consumed(f"p_o{bank}", vP)
                    em.wait("sync", vP, em.now(vP))
                    em.dma("sync", dst[128 * m:128 * (m + 1), 512 * th:512 * (th + 1)],
                           o_fm[:, :], st)
                    em.wait("vector", st, em.now(st))
        else:
            for t in range(NT):
                for n in range(4):
                    bank = n % 2
                    pe_write(f"p_o{bank}")
                    for k in range(16):
                        em.op("tensor", lambda h, t=t, n=n, k=k, bank=bank: h.matmul(
                            p_o[bank][:, :],
                            x_fm[:, 1024 * k + 128 * t:1024 * k + 128 * (t + 1)],
                            wsb[:, D * k + 512 * n:D * k + 512 * (n + 1)],
                            start=(k == 0), stop=False))
                    em.op("tensor", lambda h, t=t, n=n, bank=bank: h.matmul(
                        p_o[bank][:, :], ones1[:, 0:128],
                        bias[:, 512 * n:512 * (n + 1)],
                        start=False, stop=True), inc=tP)
                    em.wait("vector", tP, em.now(tP))
                    em.op("vector", lambda h, bank=bank: h.tensor_copy(
                        o_fm[:, :], p_o[bank][:, :]), inc=vP)
                    consumed(f"p_o{bank}", vP)
                    em.wait("sync", vP, em.now(vP))
                    em.dma("sync", v_out[128 * t:128 * (t + 1), 512 * n:512 * (n + 1)],
                           o_fm[:, :], st)
                    em.wait("vector", st, em.now(st))
    em.wait("sync", st, em.now(st))
    em.emit_and_close()
    es.close()
    return nc


def run_launchC(y2_full, inputs):
    shards = shard_rows(np.asarray(y2_full, np.float32))
    nc = build_launchC()
    base = {"ident_hbm": IDENT_BF16}
    for w, nm in (("q", "Wq"), ("k", "Wk"), ("v", "Wv")):
        base[f"W{w}"] = np.ascontiguousarray(
            np.asarray(inputs[f"L1_attn_{nm}"], np.float32).T[KAPPA_ORIG])
        base[f"b{w}"] = np.asarray(inputs[f"L1_attn_b{w}"], np.float32).reshape(1, D)
    in_maps = [dict(base, y2=s) for s in shards]
    res = run_bass_kernel_spmd(nc, in_maps, core_ids=list(range(NCORES)))
    return ([np.asarray(r["q_fm"]) for r in res.results],
            [np.asarray(r["k_fm"]) for r in res.results],
            [np.asarray(r["v_tm"]) for r in res.results])



ALIBI_SLOPES = None


def _alibi_slopes(n):
    def pow2_slopes(n):
        start = 2 ** (-2 ** (-(math.log2(n) - 3)))
        return [start * start ** i for i in range(n)]
    if math.log2(n).is_integer():
        return pow2_slopes(n)
    c = 2 ** math.floor(math.log2(n))
    return pow2_slopes(c) + _alibi_slopes(2 * c)[::2][:n - c]


def build_launchD(core):
    sA, sB = strip_list(core)[0][1], strip_list(core)[1][1]
    nc = bass.Bass("TRN2", target_bir_lowering=False)
    y_in = nc.dram_tensor("y2", [TPC, D], F32, kind="ExternalInput").ap()
    q_in = nc.dram_tensor("q_fm", [D, TPC], BF16, kind="ExternalInput").ap()
    k_in = nc.dram_tensor("k_glob", [D, S], BF16, kind="ExternalInput").ap()
    v_in = nc.dram_tensor("v_hm", [16, S, 128], BF16, kind="ExternalInput").ap()
    wo_in = nc.dram_tensor("WoT", [D, D], F32, kind="ExternalInput").ap()
    bo_in = nc.dram_tensor("bo", [1, D], F32, kind="ExternalInput").ap()
    ident_in = nc.dram_tensor("ident_hbm", [128, 128], BF16, kind="ExternalInput").ap()
    y_out = nc.dram_tensor("y3", [TPC, D], F32, kind="ExternalOutput").ap()

    es = contextlib.ExitStack()
    sb = lambda nm, shp, dt: es.enter_context(nc.sbuf_tensor(nm, shp, dt))
    ps = lambda nm, shp, dt: es.enter_context(nc.psum_tensor(nm, shp, dt))

    x_tiles = [sb(f"x{i}", [128, D], F32) for i in range(NT)]
    q_sb = sb("q_sb", [128, 16 * TPC], BF16)
    att_fm = sb("att_fm", [128, 16 * TPC], BF16)
    k_sb = sb("k_sb", [128, S], BF16)
    v_sb = sb("v_sb", [128, S], BF16)
    sc_sb = sb("sc_sb", [128, S], F32)
    p_sb = sb("p_sb", [128, S], BF16)
    pT_sb = sb("pT_sb", [128, S], BF16)
    rel = sb("rel", [128, 512], F32)
    mrel = sb("mrel", [128, 512], U16)
    negt = sb("negt", [128, 512], F32)
    srow = sb("srow", [128, 8], F32)
    wo_sb = sb("wo_sb", [128, 16 * 512], BF16)
    bo_sb = sb("bo_sb", [1, D], BF16)
    ones1 = sb("ones1", [1, 128], BF16)
    ident = sb("ident", [128, 128], BF16)

    p_s = ps("p_s", [128, 512], F32)
    p_tr = ps("p_tr", [128, 1024], BF16)
    p_ov = ps("p_ov", [128, 512], F32)
    p_pr = [ps(f"p_pr{i}", [128, 512], F32) for i in range(2)]

    em = Emit(nc)
    ld = em.sem("ld")
    lg = em.sem("lg")
    gP = em.sem("gP")
    vP = em.sem("vP")
    sP = em.sem("sP")
    tP = em.sem("tP")
    st = em.sem("st")

    guard = {}

    def pe_write(bank):
        if bank in guard:
            s_, v_ = guard[bank]
            em.wait("tensor", s_, v_)

    def consumed(bank, sem):
        guard[bank] = (sem, em.now(sem))

    for i in range(NT):
        em.dma("sync", x_tiles[i][:, :], y_in[128 * i:128 * (i + 1), :], ld)
    for h in range(16):
        em.dma("sync", q_sb[:, TPC * h:TPC * (h + 1)], q_in[128 * h:128 * (h + 1), :], ld)
    em.dma("sync", ident[:, :], ident_in, ld)
    em.wait("vector", ld, em.now(ld))
    em.op("vector", lambda h: h.memset(negt[:, :], -1e30))
    em.op("vector", lambda h: h.memset(ones1[:, :], 1.0), inc=vP)

    slopes = [s * math.sqrt(128.0) for s in _alibi_slopes(16)]
    SC = 128.0 ** -0.5

    em.wait("sync", ld, em.now(ld))
    for si, s in enumerate((sA, sB)):
        L = 512 * (s + 1)
        C = L // 128
        for hh in range(16):
            em.wait("sync", tP, em.now(tP))
            em.dma("sync", k_sb[:, 0:L], k_in[128 * hh:128 * (hh + 1), 0:L], ld)
            for kc in range(C):
                em.dma("sync", v_sb[:, 128 * kc:128 * (kc + 1)],
                       v_in[hh, 128 * kc:128 * (kc + 1), :], ld)
            kv_ready = em.now(ld)
            em.wait("tensor", ld, kv_ready)
            for qt in range(4):
                qloc = 1024 * hh + 512 * si + 128 * qt
                q0 = 512 * s + 128 * qt
                for ks in range(s + 1):
                    em.wait("gpsimd", vP, em.now(vP))
                    em.op("gpsimd", lambda h, q0=q0, ks=ks: h.iota(
                        rel[:, :], pattern=[[-1, 512]], base=q0 - 512 * ks,
                        channel_multiplier=1,
                        allow_small_or_imprecise_dtypes=True), inc=gP)
                    pe_write("p_s")
                    em.op("tensor", lambda h, qloc=qloc, ks=ks: h.matmul(
                        p_s[:, :], q_sb[:, qloc:qloc + 128],
                        k_sb[:, 512 * ks:512 * (ks + 1)],
                        start=True, stop=True), inc=tP)
                    em.wait("vector", tP, em.now(tP))
                    em.wait("vector", gP, em.now(gP))
                    em.op("vector", lambda h, ks=ks, hh=hh: h.scalar_tensor_tensor(
                        out=sc_sb[:, 512 * ks:512 * (ks + 1)], in0=rel[:, :],
                        scalar=-slopes[hh], in1=p_s[:, :],
                        op0=ALU.mult, op1=ALU.add), inc=vP)
                    consumed("p_s", vP)
                    if ks == s:
                        em.op("vector", lambda h: h.drain())
                        em.op("vector", lambda h: h.tensor_scalar(
                            out=mrel[:, :], in0=rel[:, :], scalar1=0.0,
                            scalar2=None, op0=ALU.is_lt))
                        em.op("vector", lambda h: h.drain())
                        em.op("vector", lambda h, ks=ks: h.copy_predicated(
                            sc_sb[:, 512 * ks:512 * (ks + 1)], mrel[:, :],
                            negt[:, :]), inc=vP)
                em.op("vector", lambda h: h.drain())
                em.op("vector", lambda h, L=L: h.tensor_reduce(
                    out=srow[:, 0:1], in_=sc_sb[:, 0:L], axis=AX.X, op=ALU.max))
                em.op("vector", lambda h: h.drain())
                em.op("vector", lambda h: h.tensor_scalar(
                    out=srow[:, 1:2], in0=srow[:, 0:1], scalar1=-SC, scalar2=None,
                    op0=ALU.mult), inc=vP)
                em.wait("scalar", vP, em.now(vP))
                em.op("scalar", lambda h, L=L: h.activation(
                    out=p_sb[:, 0:L], in_=sc_sb[:, 0:L], func=AF.Exp,
                    bias=srow[:, 1:2], scale=SC, accum_out=srow[:, 2:3]), inc=sP)
                em.wait("vector", sP, em.now(sP))
                em.op("vector", lambda h: h.reciprocal(srow[:, 3:4], srow[:, 2:3]))
                em.op("vector", lambda h: h.drain())
                em.op("vector", lambda h, L=L: h.tensor_scalar(
                    out=p_sb[:, 0:L], in0=p_sb[:, 0:L], scalar1=srow[:, 3:4],
                    scalar2=None, op0=ALU.mult), inc=vP)
                pn_done = em.now(vP)
                em.wait("tensor", vP, pn_done)
                for kc in range(C):
                    pe_write("p_tr")
                    em.op("tensor", lambda h, kc=kc: h.transpose(
                        p_tr[:, 0:128], p_sb[:, 128 * kc:128 * (kc + 1)],
                        ident[:, :]), inc=tP)
                    em.wait("vector", tP, em.now(tP))
                    em.op("vector", lambda h, kc=kc: h.tensor_copy(
                        pT_sb[:, 128 * kc:128 * (kc + 1)], p_tr[:, 0:128]), inc=vP)
                    consumed("p_tr", vP)
                em.wait("tensor", vP, em.now(vP))
                pe_write("p_ov")
                for kc in range(C):
                    em.op("tensor", lambda h, kc=kc, C=C: h.matmul(
                        p_ov[:, 0:128], v_sb[:, 128 * kc:128 * (kc + 1)],
                        pT_sb[:, 128 * kc:128 * (kc + 1)],
                        start=(kc == 0), stop=(kc == C - 1)),
                        inc=(tP if kc == C - 1 else None))
                em.wait("vector", tP, em.now(tP))
                em.op("vector", lambda h, qloc=qloc: h.tensor_copy(
                    att_fm[:, qloc:qloc + 128], p_ov[:, 0:128]), inc=vP)
                consumed("p_ov", vP)
    att_done = em.now(vP)
    em.wait("tensor", vP, att_done)

    for n in range(4):
        em.wait("gpsimd", tP, em.now(tP))
        for kd in range(16):
            em.dma("gpsimd", wo_sb[:, 512 * kd:512 * (kd + 1)],
                   wo_in[128 * kd:128 * (kd + 1), 512 * n:512 * (n + 1)], lg)
        if n == 0:
            em.dma("gpsimd", bo_sb[:, :], bo_in, lg)
        em.wait("tensor", lg, em.now(lg))
        for t in range(NT):
            bank = t % 2
            pe_write(f"p_pr{bank}")
            for kd in range(16):
                em.op("tensor", lambda h, t=t, kd=kd, bank=bank: h.matmul(
                    p_pr[bank][:, :], att_fm[:, 1024 * kd + 128 * t:1024 * kd + 128 * (t + 1)],
                    wo_sb[:, 512 * kd:512 * (kd + 1)],
                    start=(kd == 0), stop=False))
            em.op("tensor", lambda h, n=n, bank=bank: h.matmul(
                p_pr[bank][:, :], ones1[:, :], bo_sb[:, 512 * n:512 * (n + 1)],
                start=False, stop=True), inc=tP)
            em.wait("vector", tP, em.now(tP))
            em.op("vector", lambda h, t=t, n=n, bank=bank: h.tensor_tensor(
                out=x_tiles[t][:, 512 * n:512 * (n + 1)],
                in0=x_tiles[t][:, 512 * n:512 * (n + 1)],
                in1=p_pr[bank][:, :], op=ALU.add), inc=vP)
            em.op("vector", lambda h: h.drain())
            consumed(f"p_pr{bank}", vP)
    done = em.now(vP)
    em.wait("sync", vP, done)
    for t in range(NT):
        em.dma("sync", y_out[128 * t:128 * (t + 1), :], x_tiles[t][:, :], st)
    em.wait("sync", st, em.now(st))
    em.emit_and_close()
    es.close()
    return nc


def make_kv_global(k_list, v_list):
    k_glob = [np.empty((D, S), k_list[0].dtype) for _ in range(B)]
    v_glob = [np.empty((S, D), v_list[0].dtype) for _ in range(B)]
    for c in range(NCORES):
        (b1, s1), (b2, s2) = strip_list(c)
        k_glob[b1][:, 512 * s1:512 * (s1 + 1)] = k_list[c][:, 0:512]
        k_glob[b2][:, 512 * s2:512 * (s2 + 1)] = k_list[c][:, 512:1024]
        v_glob[b1][512 * s1:512 * (s1 + 1), :] = v_list[c][0:512, :]
        v_glob[b2][512 * s2:512 * (s2 + 1), :] = v_list[c][512:1024, :]
    v_hm = [np.ascontiguousarray(
        v.reshape(S, 16, 128).transpose(1, 0, 2)) for v in v_glob]
    return k_glob, v_hm


def run_launchD(y2_full, inputs, q_list, k_list, v_list):
    shards = shard_rows(np.asarray(y2_full, np.float32))
    k_glob, v_hm = make_kv_global(k_list, v_list)
    WoT = np.ascontiguousarray(np.asarray(inputs["L1_attn_Wo"], np.float32).T)
    bo = np.asarray(inputs["L1_attn_bo"], np.float32).reshape(1, D)
    outs = [None] * NCORES
    for j in range(4):
        nc = build_launchD(j)
        cores = [j, j + 4]
        in_maps = []
        for c in cores:
            b = c // 4
            in_maps.append({"y2": shards[c], "q_fm": q_list[c],
                            "k_glob": k_glob[b], "v_hm": v_hm[b],
                            "WoT": WoT, "bo": bo, "ident_hbm": IDENT_BF16})
        res = run_bass_kernel_spmd(nc, in_maps, core_ids=cores)
        for i, c in enumerate(cores):
            outs[c] = np.asarray(res.results[i]["y3"])
    return unshard_rows(outs)



def build_launchE1():
    nc = bass.Bass("TRN2", target_bir_lowering=False)
    wg = nc.dram_tensor("wgT_sh", [D, 1024], F32, kind="ExternalInput").ap()
    wu = nc.dram_tensor("wuT_sh", [D, 1024], F32, kind="ExternalInput").ap()
    wd = nc.dram_tensor("wdT_sh", [1024, D], F32, kind="ExternalInput").ap()
    part_out = nc.dram_tensor("part", [1, 3], F32, kind="ExternalOutput").ap()

    es = contextlib.ExitStack()
    sb = lambda nm, shp, dt: es.enter_context(nc.sbuf_tensor(nm, shp, dt))
    ps = lambda nm, shp, dt: es.enter_context(nc.psum_tensor(nm, shp, dt))
    wt = sb("wt", [128, D], F32)
    scr = [sb(f"scr{b}", [128, D], BF16) for b in range(2)]
    acc = sb("acc", [128, 48], F32)
    ones = sb("ones", [128, 1], F32)
    red = sb("red", [1, 48], F32)
    out3 = sb("out3", [1, 3], F32)
    p_r = ps("p_r", [128, 512], F32)

    em = Emit(nc)
    ld = em.sem("ld")
    vP = em.sem("vP")
    sP = em.sem("sP")
    tP = em.sem("tP")
    st = em.sem("st")

    em.op("vector", lambda h: h.memset(ones[:, :], 1.0), inc=vP)
    col = 0
    jobs = [(wg, 16, 1024), (wu, 16, 1024), (wd, 8, D)]
    bounds = []
    for (src_ap, ntiles, width) in jobs:
        c0 = col
        for i in range(ntiles):
            em.wait("sync", sP, em.now(sP))
            em.dma("sync", wt[:, 0:width], src_ap[128 * i:128 * (i + 1), :], ld)
            em.wait("scalar", ld, em.now(ld))
            em.op("scalar", lambda h, col=col, width=width, i=i: h.activation(
                out=scr[i % 2][:, 0:width], in_=wt[:, 0:width], func=AF.Abs,
                accum_out=acc[:, col:col + 1]), inc=sP)
            em.op("scalar", lambda h: h.drain())
            col += 1
        bounds.append((c0, col))
    em.wait("tensor", sP, em.now(sP))
    em.wait("tensor", vP, em.now(vP))
    em.op("tensor", lambda h: h.matmul(p_r[0:1, 0:48], ones[:, :], acc[:, 0:48],
                                       start=True, stop=True), inc=tP)
    em.wait("vector", tP, em.now(tP))
    em.op("vector", lambda h: h.tensor_copy(red[:, :], p_r[0:1, 0:48]))
    em.op("vector", lambda h: h.drain())
    for w, (c0, c1) in enumerate(bounds):
        em.op("vector", lambda h, w=w, c0=c0, c1=c1: h.tensor_reduce(
            out=out3[:, w:w + 1], in_=red[:, c0:c1], axis=AX.X, op=ALU.add))
        em.op("vector", lambda h: h.drain())
    em.op("vector", lambda h: h.tensor_scalar(
        out=out3[:, :], in0=out3[:, :], scalar1=1.0, scalar2=None,
        op0=ALU.mult), inc=vP)
    em.wait("sync", vP, em.now(vP))
    em.dma("sync", part_out, out3[:, :], st)
    em.wait("sync", st, em.now(st))
    em.emit_and_close()
    es.close()
    return nc


def build_launchE2():
    nc = bass.Bass("TRN2", target_bir_lowering=False)
    wg = nc.dram_tensor("wgT_sh", [D, 1024], F32, kind="ExternalInput").ap()
    wu = nc.dram_tensor("wuT_sh", [D, 1024], F32, kind="ExternalInput").ap()
    wd = nc.dram_tensor("wdT_sh", [1024, D], F32, kind="ExternalInput").ap()
    parts = nc.dram_tensor("parts_row", [1, 24], F32, kind="ExternalInput").ap()
    qg = nc.dram_tensor("qgT_sh", [D, 1024], BF16, kind="ExternalOutput").ap()
    qu = nc.dram_tensor("quT_sh", [D, 1024], BF16, kind="ExternalOutput").ap()
    qd = nc.dram_tensor("qdT_sh", [1024, D], BF16, kind="ExternalOutput").ap()

    es = contextlib.ExitStack()
    sb = lambda nm, shp, dt: es.enter_context(nc.sbuf_tensor(nm, shp, dt))
    ps = lambda nm, shp, dt: es.enter_context(nc.psum_tensor(nm, shp, dt))
    wt = sb("wt", [128, D], F32)
    t1 = sb("t1", [128, D], BF16)
    t2 = sb("t2", [128, D], BF16)
    qt = sb("qt", [128, D], BF16)
    prow = sb("prow", [1, 24], F32)
    mrow = sb("mrow", [1, 3], F32)
    ones1 = sb("ones1", [1, 128], F32)
    mcol = sb("mcol", [128, 3], F32)
    scol = sb("scol", [128, 3], F32)
    hcol = sb("hcol", [128, 3], F32)
    bm = sb("bm", [128, 1], F32)
    bp = sb("bp", [128, 1], F32)
    p_t = ps("p_t", [128, 512], F32)

    em = Emit(nc)
    ld = em.sem("ld")
    vP = em.sem("vP")
    sP = em.sem("sP")
    tP = em.sem("tP")
    st = em.sem("st")

    em.dma("sync", prow[:, :], parts, ld)
    em.wait("vector", ld, em.now(ld))
    em.op("vector", lambda h: h.memset(ones1[:, :], 1.0))
    em.op("vector", lambda h: h.memset(bm[:, :], -0.5))
    em.op("vector", lambda h: h.memset(bp[:, :], 0.5))
    for w in range(3):
        em.op("vector", lambda h, w=w: h.tensor_reduce(
            out=mrow[:, w:w + 1], in_=prow[:, 8 * w:8 * (w + 1)], axis=AX.X,
            op=ALU.add))
        em.op("vector", lambda h: h.drain())
    em.op("vector", lambda h: h.tensor_scalar(
        out=mrow[:, :], in0=mrow[:, :], scalar1=1.0 / (8192.0 * D), scalar2=1e-5,
        op0=ALU.mult, op1=ALU.max), inc=vP)
    em.wait("tensor", vP, em.now(vP))
    em.op("tensor", lambda h: h.matmul(
        p_t[:, 0:3], ones1[:, :], mrow[:, :], start=True, stop=True), inc=tP)
    em.wait("vector", tP, em.now(tP))
    em.op("vector", lambda h: h.tensor_copy(mcol[:, :], p_t[:, 0:3]))
    em.op("vector", lambda h: h.drain())
    em.op("vector", lambda h: h.reciprocal(scol[:, :], mcol[:, :]))
    em.op("vector", lambda h: h.drain())
    em.op("vector", lambda h: h.tensor_scalar(
        out=hcol[:, :], in0=mcol[:, :], scalar1=0.5, scalar2=None,
        op0=ALU.mult), inc=vP)
    bcast_done = em.now(vP)
    em.wait("scalar", vP, bcast_done)

    jobs = [(wg, qg, 16, 1024, 0), (wu, qu, 16, 1024, 1), (wd, qd, 8, D, 2)]
    for (src_ap, dst_ap, ntiles, width, w) in jobs:
        for i in range(ntiles):
            em.wait("sync", sP, em.now(sP))
            em.dma("sync", wt[:, 0:width], src_ap[128 * i:128 * (i + 1), :], ld)
            em.wait("scalar", ld, em.now(ld))
            em.op("scalar", lambda h, w=w, width=width: h.activation(
                out=t1[:, 0:width], in_=wt[:, 0:width], func=AF.Sign,
                bias=bm[:, :], scale=scol[:, w:w + 1]))
            em.op("scalar", lambda h: h.drain())
            em.op("scalar", lambda h, w=w, width=width: h.activation(
                out=t2[:, 0:width], in_=wt[:, 0:width], func=AF.Sign,
                bias=bp[:, :], scale=scol[:, w:w + 1]), inc=sP)
            em.wait("vector", sP, em.now(sP))
            em.wait("vector", st, em.now(st))
            em.op("vector", lambda h, width=width: h.tensor_tensor(
                out=qt[:, 0:width], in0=t1[:, 0:width], in1=t2[:, 0:width],
                op=ALU.add))
            em.op("vector", lambda h: h.drain())
            em.op("vector", lambda h, w=w, width=width: h.tensor_scalar(
                out=qt[:, 0:width], in0=qt[:, 0:width], scalar1=hcol[:, w:w + 1],
                scalar2=None, op0=ALU.mult), inc=vP)
            em.wait("sync", vP, em.now(vP))
            em.dma("sync", dst_ap[128 * i:128 * (i + 1), :], qt[:, 0:width], st)
    em.wait("sync", st, em.now(st))
    em.emit_and_close()
    es.close()
    return nc


def run_launchE12(inputs):
    wgT = np.asarray(inputs["L1_ffn_w_gate"], np.float32).T
    wuT = np.asarray(inputs["L1_ffn_w_up"], np.float32).T
    wdT = np.asarray(inputs["L1_ffn_w_down"], np.float32).T
    sh = lambda c: {"wgT_sh": np.ascontiguousarray(wgT[:, 1024 * c:1024 * (c + 1)]),
                    "wuT_sh": np.ascontiguousarray(wuT[:, 1024 * c:1024 * (c + 1)]),
                    "wdT_sh": np.ascontiguousarray(wdT[1024 * c:1024 * (c + 1), :])}
    nc1 = build_launchE1()
    res1 = run_bass_kernel_spmd(nc1, [sh(c) for c in range(NCORES)],
                                core_ids=list(range(NCORES)))
    parts = np.stack([np.asarray(r["part"])[0] for r in res1.results])
    parts_row = np.ascontiguousarray(parts.T.reshape(1, 24))
    nc2 = build_launchE2()
    in_maps = [dict(sh(c), parts_row=parts_row) for c in range(NCORES)]
    res2 = run_bass_kernel_spmd(nc2, in_maps, core_ids=list(range(NCORES)))
    qgT = np.concatenate([np.asarray(r["qgT_sh"]) for r in res2.results], axis=1)
    quT = np.concatenate([np.asarray(r["quT_sh"]) for r in res2.results], axis=1)
    qdT = np.concatenate([np.asarray(r["qdT_sh"]) for r in res2.results], axis=0)
    return qgT, quT, qdT



def build_launchE3():
    nc = bass.Bass("TRN2", target_bir_lowering=False)
    y_in = nc.dram_tensor("y3", [TPC, D], F32, kind="ExternalInput").ap()
    qg_in = nc.dram_tensor("qgT", [D, 8192], BF16, kind="ExternalInput").ap()
    qu_in = nc.dram_tensor("quT", [D, 8192], BF16, kind="ExternalInput").ap()
    qd_in = nc.dram_tensor("qdT", [8192, D], BF16, kind="ExternalInput").ap()
    ident_in = nc.dram_tensor("ident_hbm", [128, 128], BF16, kind="ExternalInput").ap()
    y_out = nc.dram_tensor("y4", [TPC, D], F32, kind="ExternalOutput").ap()

    es = contextlib.ExitStack()
    sb = lambda nm, shp, dt: es.enter_context(nc.sbuf_tensor(nm, shp, dt))
    ps = lambda nm, shp, dt: es.enter_context(nc.psum_tensor(nm, shp, dt))

    x_tiles = [sb(f"x{i}", [128, D], F32) for i in range(NT)]
    h_tg = [sb(f"h{i}", [128, 8192], BF16) for i in range(4)]
    xq_fm = sb("xq_fm", [128, 16 * TPC], BF16)
    ident = sb("ident", [128, 128], BF16)
    xn = sb("xn", [128, D], BF16)
    xn2 = sb("xn2", [128, D], BF16)
    zb = sb("zb", [128, 512], F32)
    zi = sb("zi", [128, 512], I32)
    stats = sb("stats", [128, 32], F32)
    gact = sb("gact", [128, 512], BF16)
    wg_sb = sb("wg_sb", [128, 16 * 512], BF16)
    wu_sb = sb("wu_sb", [128, 16 * 512], BF16)
    wdb = [sb(f"wdb{b}", [128, 512], BF16) for b in range(2)]
    acc16 = sb("acc16", [128, 16], F32)

    p_g = ps("p_g", [128, 512], F32)
    p_u = ps("p_u", [128, 512], F32)
    p_tr = ps("p_tr", [128, 1024], BF16)
    p_y = [ps(f"p_y{n}", [128, 512], F32) for n in range(4)]

    em = Emit(nc)
    ld = em.sem("ld")
    lw = [em.sem("lw0"), em.sem("lw1")]
    vP = em.sem("vP")
    sP = em.sem("sP")
    tP = em.sem("tP")
    st = em.sem("st")

    guard = {}

    def pe_write(bank):
        if bank in guard:
            s_, v_ = guard[bank]
            em.wait("tensor", s_, v_)

    def consumed(bank, sem):
        guard[bank] = (sem, em.now(sem))

    for i in range(NT):
        em.dma("sync", x_tiles[i][:, :], y_in[128 * i:128 * (i + 1), :], ld)
    em.dma("sync", ident[:, :], ident_in, ld)
    ld_all = em.now(ld)
    em.wait("vector", ld, ld_all)
    em.wait("scalar", ld, ld_all)
    em.wait("tensor", ld, ld_all)

    for t in range(NT):
        em.wait("scalar", vP, em.now(vP))
        emit_ln(em, [x_tiles[t]], [xn], stats, None, [0], vP, sP, eps=1e-5)
        em.wait("scalar", vP, em.now(vP))
        emit_ln(em, [xn], [xn2], stats[:, 8:], None, [0], vP, sP, rms=True,
                eps=1e-6)
        em.op("vector", lambda h: h.drain())
        em.op("vector", lambda h: h.tensor_reduce(
            out=stats[:, 16:17], in_=xn2[:, :], axis=AX.X, op=ALU.max,
            apply_absolute_value=True))
        em.op("vector", lambda h: h.drain())
        em.op("vector", lambda h: h.tensor_scalar(
            out=stats[:, 16:17], in0=stats[:, 16:17], scalar1=1e-5,
            scalar2=1.0 / 127.0, op0=ALU.max, op1=ALU.mult))
        em.op("vector", lambda h: h.drain())
        em.op("vector", lambda h: h.reciprocal(stats[:, 17:18], stats[:, 16:17]))
        em.op("vector", lambda h: h.drain())
        for sl in range(4):
            em.op("vector", lambda h, sl=sl: h.tensor_scalar(
                out=zb[:, :], in0=xn2[:, 512 * sl:512 * (sl + 1)],
                scalar1=stats[:, 17:18], scalar2=None, op0=ALU.mult))
            em.op("vector", lambda h: h.drain())
            em.op("vector", lambda h: h.tensor_copy(zi[:, :], zb[:, :]))
            em.op("vector", lambda h: h.drain())
            em.op("vector", lambda h, sl=sl: h.tensor_scalar(
                out=xn2[:, 512 * sl:512 * (sl + 1)], in0=zi[:, :],
                scalar1=stats[:, 16:17], scalar2=None, op0=ALU.mult),
                inc=(vP if sl == 3 else None))
            em.op("vector", lambda h: h.drain())
        em.wait("tensor", vP, em.now(vP))
        for k in range(16):
            pe_write("p_tr")
            em.op("tensor", lambda h, k=k: h.transpose(
                p_tr[:, 0:128], xn2[:, 128 * k:128 * (k + 1)], ident[:, :]),
                inc=tP)
            em.wait("vector", tP, em.now(tP))
            em.op("vector", lambda h, k=k, t=t: h.tensor_copy(
                xq_fm[:, 1024 * k + 128 * t:1024 * k + 128 * (t + 1)],
                p_tr[:, 0:128]), inc=vP)
            consumed("p_tr", vP)
    prep_done = em.now(vP)
    em.wait("tensor", vP, prep_done)

    last_wd_tok = {0: None, 1: None}
    for tg in range(2):
        tiles = list(range(4 * tg, 4 * tg + 4))
        for n in range(16):
            em.wait("sync", tP, em.now(tP))
            for k in range(16):
                em.dma("sync", wg_sb[:, 512 * k:512 * (k + 1)],
                       qg_in[128 * k:128 * (k + 1), 512 * n:512 * (n + 1)], ld)
                em.dma("sync", wu_sb[:, 512 * k:512 * (k + 1)],
                       qu_in[128 * k:128 * (k + 1), 512 * n:512 * (n + 1)], ld)
            em.wait("tensor", ld, em.now(ld))
            for ti, t in enumerate(tiles):
                pe_write("p_g")
                pe_write("p_u")
                for k in range(16):
                    em.op("tensor", lambda h, k=k, t=t: h.matmul(
                        p_g[:, :], xq_fm[:, 1024 * k + 128 * t:1024 * k + 128 * (t + 1)],
                        wg_sb[:, 512 * k:512 * (k + 1)],
                        start=(k == 0), stop=(k == 15)))
                for k in range(16):
                    em.op("tensor", lambda h, k=k, t=t: h.matmul(
                        p_u[:, :], xq_fm[:, 1024 * k + 128 * t:1024 * k + 128 * (t + 1)],
                        wu_sb[:, 512 * k:512 * (k + 1)],
                        start=(k == 0), stop=(k == 15)),
                        inc=(tP if k == 15 else None))
                em.wait("scalar", tP, em.now(tP))
                em.wait("scalar", vP, em.now(vP))
                em.op("scalar", lambda h: h.activation(
                    out=gact[:, :], in_=p_g[:, :], func=AF.Silu), inc=sP)
                consumed("p_g", sP)
                em.wait("vector", sP, em.now(sP))
                em.op("vector", lambda h, t=t, ti=ti, n=n: h.tensor_tensor(
                    out=h_tg[ti][:, 512 * n:512 * (n + 1)], in0=p_u[:, :],
                    in1=gact[:, :], op=ALU.mult), inc=vP)
                consumed("p_u", vP)
        for ti, t in enumerate(tiles):
            em.wait("scalar", vP, em.now(vP))
            for sl in range(16):
                em.op("scalar", lambda h, ti=ti, sl=sl: h.activation(
                    out=gact[:, :], in_=h_tg[ti][:, 512 * sl:512 * (sl + 1)],
                    func=AF.Square, accum_out=acc16[:, sl:sl + 1]),
                    inc=(sP if sl == 15 else None))
                em.op("scalar", lambda h: h.drain())
            em.wait("vector", sP, em.now(sP))
            em.op("vector", lambda h: h.tensor_reduce(
                out=stats[:, 20:21], in_=acc16[:, :], axis=AX.X, op=ALU.add))
            em.op("vector", lambda h: h.drain())
            em.op("vector", lambda h: h.tensor_scalar(
                out=stats[:, 21:22], in0=stats[:, 20:21], scalar1=1.0 / 8192.0,
                scalar2=1e-6, op0=ALU.mult, op1=ALU.add))
            em.op("vector", lambda h: h.drain())
            em.op("vector", lambda h: h.reciprocal(stats[:, 22:23], stats[:, 21:22]),
                  inc=vP)
            em.wait("scalar", vP, em.now(vP))
            em.op("scalar", lambda h: h.activation(
                out=stats[:, 22:23], in_=stats[:, 22:23], func=AF.Sqrt), inc=sP)
            em.wait("vector", sP, em.now(sP))
            em.op("vector", lambda h: h.drain())
            em.op("vector", lambda h, ti=ti: h.tensor_reduce(
                out=stats[:, 23:24], in_=h_tg[ti][:, :], axis=AX.X, op=ALU.max,
                apply_absolute_value=True))
            em.op("vector", lambda h: h.drain())
            em.op("vector", lambda h: h.tensor_tensor(
                out=stats[:, 24:25], in0=stats[:, 23:24], in1=stats[:, 22:23],
                op=ALU.mult))
            em.op("vector", lambda h: h.drain())
            em.op("vector", lambda h: h.tensor_scalar(
                out=stats[:, 24:25], in0=stats[:, 24:25], scalar1=1e-5,
                scalar2=None, op0=ALU.max))
            em.op("vector", lambda h: h.drain())
            em.op("vector", lambda h: h.tensor_scalar(
                out=stats[:, 25:26], in0=stats[:, 24:25], scalar1=1.0 / 127.0,
                scalar2=None, op0=ALU.mult))
            em.op("vector", lambda h: h.drain())
            em.op("vector", lambda h: h.reciprocal(stats[:, 26:27], stats[:, 25:26]))
            em.op("vector", lambda h: h.drain())
            em.op("vector", lambda h: h.tensor_tensor(
                out=stats[:, 27:28], in0=stats[:, 26:27], in1=stats[:, 22:23],
                op=ALU.mult))
            em.op("vector", lambda h: h.drain())
            for sl in range(16):
                em.op("vector", lambda h, ti=ti, sl=sl: h.tensor_scalar(
                    out=zb[:, :], in0=h_tg[ti][:, 512 * sl:512 * (sl + 1)],
                    scalar1=stats[:, 27:28], scalar2=None, op0=ALU.mult))
                em.op("vector", lambda h: h.drain())
                em.op("vector", lambda h: h.tensor_copy(zi[:, :], zb[:, :]))
                em.op("vector", lambda h: h.drain())
                em.op("vector", lambda h, ti=ti, sl=sl: h.tensor_scalar(
                    out=h_tg[ti][:, 512 * sl:512 * (sl + 1)], in0=zi[:, :],
                    scalar1=stats[:, 25:26], scalar2=None, op0=ALU.mult),
                    inc=(vP if sl == 15 else None))
                em.op("vector", lambda h: h.drain())
            em.wait("tensor", vP, em.now(vP))
            for k3 in range(64):
                pe_write("p_tr")
                em.op("tensor", lambda h, ti=ti, k3=k3: h.transpose(
                    p_tr[:, 0:128], h_tg[ti][:, 128 * k3:128 * (k3 + 1)],
                    ident[:, :]), inc=tP)
                em.wait("vector", tP, em.now(tP))
                em.op("vector", lambda h, ti=ti, k3=k3: h.tensor_copy(
                    h_tg[ti][:, 128 * k3:128 * (k3 + 1)], p_tr[:, 0:128]), inc=vP)
                consumed("p_tr", vP)
        tr3_done = em.now(vP)
        em.wait("tensor", vP, tr3_done)
        for n in range(4):
            for ti in range(4):
                pe_write(f"p_y{ti}")
            for k3 in range(64):
                b = k3 % 2
                if last_wd_tok[b] is not None:
                    em.wait("sync", tP, last_wd_tok[b])
                em.dma("sync", wdb[b][:, :],
                       qd_in[128 * k3:128 * (k3 + 1), 512 * n:512 * (n + 1)], lw[b])
                em.wait("tensor", lw[b], em.now(lw[b]))
                for ti in range(4):
                    em.op("tensor", lambda h, ti=ti, k3=k3, b=b: h.matmul(
                        p_y[ti][:, :], h_tg[ti][:, 128 * k3:128 * (k3 + 1)],
                        wdb[b][:, :], start=(k3 == 0), stop=(k3 == 63)),
                        inc=(tP if ti == 3 else None))
                last_wd_tok[b] = em.now(tP)
            em.wait("vector", tP, em.now(tP))
            for ti, t in enumerate(tiles):
                em.op("vector", lambda h, t=t, ti=ti, n=n: h.tensor_tensor(
                    out=x_tiles[t][:, 512 * n:512 * (n + 1)],
                    in0=x_tiles[t][:, 512 * n:512 * (n + 1)],
                    in1=p_y[ti][:, :], op=ALU.add), inc=(vP if ti == 3 else None))
                consumed(f"p_y{ti}", vP)
            em.op("vector", lambda h: h.drain())
    done = em.now(vP)
    em.wait("sync", vP, done)
    for t in range(NT):
        em.dma("sync", y_out[128 * t:128 * (t + 1), :], x_tiles[t][:, :], st)
    em.wait("sync", st, em.now(st))
    em.emit_and_close()
    es.close()
    return nc


def run_launchE3(y3_full, qgT, quT, qdT):
    shards = shard_rows(np.asarray(y3_full, np.float32))
    nc = build_launchE3()
    in_maps = [{"y3": s, "qgT": qgT, "quT": quT, "qdT": qdT,
                "ident_hbm": IDENT_BF16} for s in shards]
    res = run_bass_kernel_spmd(nc, in_maps, core_ids=list(range(NCORES)))
    return unshard_rows([np.asarray(r["y4"]) for r in res.results])


# version 03
# speedup vs baseline: 1.0025x; 1.0025x over previous
import contextlib
import math
import numpy as np
import ml_dtypes
import concourse.bass as bass
import concourse.mybir as mybir
from concourse.bass_utils import run_bass_kernel_spmd

F32 = mybir.dt.float32
BF16 = mybir.dt.bfloat16
I32 = mybir.dt.int32
U16 = mybir.dt.uint16
AF = mybir.ActivationFunctionType
ALU = mybir.AluOpType
AX = mybir.AxisListType

NCORES = 8
B, S, D = 2, 4096, 2048
STRIP = 512
TPC = 1024
NT = TPC // 128
H0 = 16
HD = 128
NSTRIPS = S // STRIP

ENGINES = ["sync", "vector", "scalar", "tensor", "gpsimd"]
SERIALIZE = False


def strip_list(core):
    b, j = divmod(core, 4)
    return [(b, j), (b, 7 - j)]


def shard_rows(x):
    out = []
    for c in range(NCORES):
        (b1, s1), (b2, s2) = strip_list(c)
        out.append(np.ascontiguousarray(np.concatenate(
            [x[b1, s1 * STRIP:(s1 + 1) * STRIP], x[b2, s2 * STRIP:(s2 + 1) * STRIP]], axis=0)))
    return out


def unshard_rows(parts, dtype=np.float32):
    x = np.empty((B, S, D), dtype)
    for c in range(NCORES):
        (b1, s1), (b2, s2) = strip_list(c)
        x[b1, s1 * STRIP:(s1 + 1) * STRIP] = parts[c][:STRIP]
        x[b2, s2 * STRIP:(s2 + 1) * STRIP] = parts[c][STRIP:]
    return x


class Emit:

    def __init__(self, nc):
        self.nc = nc
        self.ops = {e: [] for e in ENGINES}
        self.sems = {}
        self.counts = {}
        self._es = contextlib.ExitStack()

    def sem(self, name):
        self.sems[name] = self._es.enter_context(self.nc.semaphore(name))
        self.counts[name] = 0
        return name

    def op(self, eng, fn, inc=None, inc_by=1):
        if inc is not None:
            self.counts[inc] += inc_by
        self.ops[eng].append(("op", fn, inc, inc_by))

    def dma(self, eng, out, in_, inc, **kw):
        self.op(eng, lambda h, o=out, i=in_, kw=kw: h.dma_start(out=o, in_=i, **kw),
                inc=inc, inc_by=16)

    def wait(self, eng, sem, val=None):
        v = self.counts[sem] if val is None else val
        self.ops[eng].append(("wait", sem, v, 0))

    def now(self, sem):
        return self.counts[sem]

    def barrier(self, vP, sP, tP):
        v, s_, t = self.counts[vP], self.counts[sP], self.counts[tP]
        self.wait("vector", sP, s_); self.wait("vector", tP, t)
        self.wait("scalar", vP, v); self.wait("scalar", tP, t)
        self.wait("tensor", vP, v); self.wait("tensor", sP, s_)

    def emit_and_close(self):
        with self.nc.Block() as block:
            for eng in ENGINES:
                ops = self.ops[eng]
                if not ops:
                    continue

                def body(h, ops=ops):
                    for rec in ops:
                        if rec[0] == "wait":
                            h.wait_ge(self.sems[rec[1]], rec[2])
                        else:
                            _, fn, inc, by = rec
                            inst = fn(h)
                            if inc is not None:
                                inst.then_inc(self.sems[inc], by)
                getattr(block, eng)(body)
        self._es.close()



def emit_ln(em, x_tiles, xln_tiles, stats, sq_scratch, tiles, vP, sP,
            out_dtype=BF16, rms=False, eps=1e-5):
    n = len(tiles)
    c = 1.0 / D
    sums = lambda i: stats[:, i:i + 1]
    for k, i in enumerate(tiles):
        if not rms:
            em.op("vector", lambda h, i=i, k=k: h.tensor_reduce(
                out=stats[:, k:k + 1], in_=x_tiles[i][:, :], axis=AX.X, op=ALU.add))
        em.op("scalar", lambda h, i=i, k=k: h.activation(
            out=xln_tiles[i][:, :], in_=x_tiles[i][:, :], func=AF.Square,
            accum_out=stats[:, n + k:n + k + 1]),
            inc=(sP if k == n - 1 else None))
    sq_done = em.now(sP)
    em.op("vector", lambda h: h.drain())
    em.wait("vector", sP, sq_done)
    if rms:
        em.op("vector", lambda h: h.tensor_scalar(
            out=stats[:, 3 * n:4 * n], in0=stats[:, n:2 * n], scalar1=c, scalar2=eps,
            op0=ALU.mult, op1=ALU.add))
    else:
        em.op("vector", lambda h: h.tensor_scalar(
            out=stats[:, 2 * n:3 * n], in0=stats[:, 0:n], scalar1=c,
            scalar2=None, op0=ALU.mult))
        em.op("vector", lambda h: h.drain())
        em.op("vector", lambda h: h.tensor_tensor(
            out=stats[:, 4 * n:5 * n], in0=stats[:, 2 * n:3 * n],
            in1=stats[:, 2 * n:3 * n], op=ALU.mult))
        em.op("vector", lambda h: h.drain())
        em.op("vector", lambda h: h.scalar_tensor_tensor(
            out=stats[:, 3 * n:4 * n], in0=stats[:, n:2 * n], scalar=c,
            in1=stats[:, 4 * n:5 * n], op0=ALU.mult, op1=ALU.subtract))
        em.op("vector", lambda h: h.drain())
        em.op("vector", lambda h: h.tensor_scalar(
            out=stats[:, 3 * n:4 * n], in0=stats[:, 3 * n:4 * n], scalar1=eps,
            scalar2=None, op0=ALU.add))
    em.op("vector", lambda h: h.drain())
    em.op("vector", lambda h: h.reciprocal(stats[:, 4 * n:5 * n], stats[:, 3 * n:4 * n]),
          inc=vP)
    rcp_done = em.now(vP)
    em.wait("scalar", vP, rcp_done)
    em.op("scalar", lambda h: h.activation(
        out=stats[:, 4 * n:5 * n], in_=stats[:, 4 * n:5 * n], func=AF.Sqrt), inc=sP)
    rstd_done = em.now(sP)
    em.wait("vector", sP, rstd_done)
    for k, i in enumerate(tiles):
        last = (sP if False else (vP if k == n - 1 else None))
        if rms:
            em.op("vector", lambda h, i=i, k=k: h.tensor_scalar(
                out=xln_tiles[i][:, :], in0=x_tiles[i][:, :],
                scalar1=stats[:, 4 * n + k:4 * n + k + 1], scalar2=None, op0=ALU.mult),
                inc=last)
        else:
            em.op("vector", lambda h, i=i, k=k: h.tensor_scalar(
                out=xln_tiles[i][:, :], in0=x_tiles[i][:, :],
                scalar1=stats[:, 2 * n + k:2 * n + k + 1],
                scalar2=stats[:, 4 * n + k:4 * n + k + 1],
                op0=ALU.subtract, op1=ALU.mult), inc=last)
    return em.now(vP)



def build_launchA(debug=False, stage=None, stageB=None):
    nc = bass.Bass("TRN2", target_bir_lowering=False)
    x_in = nc.dram_tensor("x", [TPC, D], F32, kind="ExternalInput").ap()
    ident_in = nc.dram_tensor("ident_hbm", [128, 128], BF16, kind="ExternalInput").ap()
    y_out = nc.dram_tensor("y", [TPC, D], F32, kind="ExternalOutput").ap()
    if debug:
        dbg_xfm = nc.dram_tensor("dbg_xfm", [128, H0 * STRIP], BF16, kind="ExternalOutput").ap()
        dbg_yfm = nc.dram_tensor("dbg_yfm", [128, H0 * STRIP], BF16, kind="ExternalOutput").ap()
        dbg_xln = nc.dram_tensor("dbg_xln", [128, D], BF16, kind="ExternalOutput").ap()
        dbg_sn1 = nc.dram_tensor("dbg_sn1", [128, H0 * STRIP], BF16, kind="ExternalOutput").ap()
        dbg_sn2 = nc.dram_tensor("dbg_sn2", [128, H0 * STRIP], BF16, kind="ExternalOutput").ap()
        dbg_id0 = nc.dram_tensor("dbg_id0", [128, 128], BF16, kind="ExternalOutput").ap()
        dbg_id1 = nc.dram_tensor("dbg_id1", [128, 128], BF16, kind="ExternalOutput").ap()

    es = contextlib.ExitStack()
    sb = lambda nm, shp, dt: es.enter_context(nc.sbuf_tensor(nm, shp, dt))
    ps = lambda nm, shp, dt: es.enter_context(nc.psum_tensor(nm, shp, dt))

    x_tiles = [sb(f"x{i}", [128, D], F32) for i in range(NT)]
    xln = [sb(f"xln{i}", [128, D], BF16) for i in range(NT)]
    stats = sb("stats", [128, 64], F32)
    sqs = sb("sqs", [128, D], BF16)
    ident = sb("ident", [128, 128], BF16)
    x_fm = sb("x_fm", [128, H0 * STRIP], BF16)
    snap1 = sb("snap1", [128, H0 * STRIP], BF16) if debug else None
    snap2 = sb("snap2", [128, H0 * STRIP], BF16) if debug else None
    idsnap = sb("idsnap", [128, 128], BF16) if debug else None
    y_fm = sb("y_fm", [128, H0 * STRIP], BF16)
    g_cmp = sb("g_cmp", [128, 256], BF16)
    gt = sb("gt", [128, 256], BF16)
    p_bf = sb("p_bf", [128, 512], BF16)
    pT = sb("pT", [128, 512], BF16)
    smax = sb("smax", [128, 8], F32)

    pt_a = ps("pt_a", [128, 1024], BF16)
    pt_b = ps("pt_b", [128, 1024], BF16)
    p_sc = ps("p_sc", [128, 512], F32)
    p_gt = ps("p_gt", [128, 1024], BF16)
    p_pT = ps("p_pT", [128, 1024], BF16)
    p_o = ps("p_o", [128, 512], F32)

    em = Emit(nc)
    ld = em.sem("ld")
    vP = em.sem("vP")
    sP = em.sem("sP")
    tP = em.sem("tP")
    st = em.sem("st")

    guard = {}

    def pe_write(bank):
        if bank in guard:
            sem, val = guard[bank]
            em.wait("tensor", sem, val)

    def consumed(bank, sem):
        guard[bank] = (sem, em.now(sem))

    for i in range(NT):
        em.dma("sync", x_tiles[i][:, :], x_in[128 * i:128 * (i + 1), :], ld)
    em.dma("sync", ident[:, :], ident_in, ld)
    ld_all = em.now(ld)

    SC = HD ** -0.5

    nstrips = 2 if stage is None else 1
    nheads = H0 if stage is None else min(stage % 100, H0)
    do_back = stage is None or stage >= 100
    for s in range(nstrips):
        t0 = 4 * s
        em.wait("vector", ld, ld_all)
        em.wait("scalar", ld, ld_all)
        if s == 1 and stageB == 13:
            break
        if s == 1 and stageB in (11, 12):
            if stageB == 11:
                for k, i in enumerate(range(t0, t0 + 4)):
                    em.op("vector", lambda h, i=i, k=k: h.tensor_reduce(
                        out=stats[:, k:k + 1], in_=x_tiles[i][:, :], axis=AX.X, op=ALU.add))
            else:
                for k, i in enumerate(range(t0, t0 + 4)):
                    em.op("scalar", lambda h, i=i, k=k: h.activation(
                        out=sqs[:, :], in_=x_tiles[i][:, :], func=AF.Square,
                        accum_out=stats[:, 4 + k:5 + k]))
            break
        ln_done = emit_ln(em, x_tiles, xln, stats, sqs, list(range(t0, t0 + 4)), vP, sP)
        if SERIALIZE:
            em.barrier(vP, sP, tP)
        if s == 1 and stageB is not None and stageB < 2:
            break
        em.wait("tensor", ld, ld_all)
        em.wait("tensor", vP, ln_done)
        em.op("vector", lambda h: h.memset(y_fm[:, :], 0.0), inc=vP)
        for t in range(4):
            for grp in range(4):
                pt = pt_a if grp % 2 == 0 else pt_b
                key = "pt_a" if grp % 2 == 0 else "pt_b"
                pe_write(key)
                for hh in range(4):
                    h_ = grp * 4 + hh
                    em.op("tensor", lambda h, t=t, h_=h_, pt=pt, hh=hh, t0=t0: h.transpose(
                        pt[:, 128 * hh:128 * (hh + 1)],
                        xln[t0 + t][:, 128 * h_:128 * (h_ + 1)], ident[:, :]),
                        inc=(tP if hh == 3 else None))
                em.wait("vector", tP, em.now(tP))
                for hh in range(4):
                    h_ = grp * 4 + hh
                    em.op("vector", lambda h, t=t, h_=h_, pt=pt, hh=hh: h.tensor_copy(
                        x_fm[:, 512 * h_ + 128 * t:512 * h_ + 128 * (t + 1)],
                        pt[:, 128 * hh:128 * (hh + 1)]),
                        inc=(vP if hh == 3 else None))
                consumed(key, vP)
        em.op("vector", lambda h: h.drain())
        if debug and s == 0:
            em.op("vector", lambda h: h.tensor_copy(snap1[:, :], x_fm[:, :]), inc=vP)
            em.op("vector", lambda h: h.tensor_copy(idsnap[:, :], ident[:, :]), inc=vP)
        fm_ready = em.now(vP)
        em.wait("tensor", vP, fm_ready)
        if SERIALIZE:
            em.barrier(vP, sP, tP)
        if s == 1 and stageB is not None and stageB < 3:
            break

        for hh in range(nheads):
            par = hh % 2
            em.op("vector", lambda h, hh=hh, par=par: h.tensor_copy(
                g_cmp[:, :], x_fm[:, 512 * hh + par:512 * (hh + 1):2]), inc=vP)
            em.wait("tensor", vP, em.now(vP))
            for kc in range(2):
                pe_write("p_gt")
                em.op("tensor", lambda h, kc=kc: h.transpose(
                    p_gt[:, 0:128], g_cmp[:, 128 * kc:128 * (kc + 1)], ident[:, :]),
                    inc=tP)
                em.wait("vector", tP, em.now(tP))
                em.op("vector", lambda h, kc=kc: h.tensor_copy(
                    gt[:, 128 * kc:128 * (kc + 1)], p_gt[:, 0:128]), inc=vP)
                consumed("p_gt", vP)
            for qc in range(2):
                pe_write("p_sc")
                em.op("tensor", lambda h, qc=qc: h.matmul(
                    p_sc[:, 0:256], g_cmp[:, 128 * qc:128 * (qc + 1)], g_cmp[:, :],
                    start=True, stop=True), inc=tP)
                em.wait("vector", tP, em.now(tP))
                em.op("vector", lambda h, qc=qc: h.tensor_reduce(
                    out=smax[:, qc:qc + 1], in_=p_sc[:, 0:256], axis=AX.X, op=ALU.max))
                em.op("vector", lambda h: h.drain())
                em.op("vector", lambda h, qc=qc: h.tensor_scalar(
                    out=smax[:, 2 + qc:3 + qc], in0=smax[:, qc:qc + 1],
                    scalar1=-SC, scalar2=None, op0=ALU.mult), inc=vP)
                em.wait("scalar", vP, em.now(vP))
                em.op("scalar", lambda h, qc=qc: h.activation(
                    out=p_bf[:, 256 * qc:256 * (qc + 1)], in_=p_sc[:, 0:256], func=AF.Exp,
                    bias=smax[:, 2 + qc:3 + qc], scale=SC,
                    accum_out=smax[:, 4 + qc:5 + qc]), inc=sP)
                consumed("p_sc", sP)
                em.wait("vector", sP, em.now(sP))
                em.op("vector", lambda h, qc=qc: h.reciprocal(
                    smax[:, 6 + qc:7 + qc], smax[:, 4 + qc:5 + qc]))
                em.op("vector", lambda h: h.drain())
                em.op("vector", lambda h, qc=qc: h.tensor_scalar(
                    out=p_bf[:, 256 * qc:256 * (qc + 1)],
                    in0=p_bf[:, 256 * qc:256 * (qc + 1)],
                    scalar1=smax[:, 6 + qc:7 + qc], scalar2=None,
                    op0=ALU.mult), inc=vP)
                em.wait("tensor", vP, em.now(vP))
                for kc in range(2):
                    pe_write("p_pT")
                    em.op("tensor", lambda h, qc=qc, kc=kc: h.transpose(
                        p_pT[:, 0:128],
                        p_bf[:, 256 * qc + 128 * kc:256 * qc + 128 * (kc + 1)],
                        ident[:, :]), inc=tP)
                    em.wait("vector", tP, em.now(tP))
                    em.op("vector", lambda h, qc=qc, kc=kc: h.tensor_copy(
                        pT[:, 256 * kc + 128 * qc:256 * kc + 128 * (qc + 1)],
                        p_pT[:, 0:128]), inc=vP)
                    consumed("p_pT", vP)
            em.wait("tensor", vP, em.now(vP))
            pe_write("p_o")
            for kc in range(2):
                em.op("tensor", lambda h, kc=kc: h.matmul(
                    p_o[:, 0:256], gt[:, 128 * kc:128 * (kc + 1)],
                    pT[:, 256 * kc:256 * (kc + 1)],
                    start=(kc == 0), stop=(kc == 1)),
                    inc=(tP if kc == 1 else None))
            em.wait("vector", tP, em.now(tP))
            em.op("vector", lambda h, hh=hh, par=par: h.tensor_copy(
                y_fm[:, 512 * hh + par:512 * (hh + 1):2], p_o[:, 0:256]), inc=vP)
            consumed("p_o", vP)
            if SERIALIZE:
                em.barrier(vP, sP, tP)
        em.op("vector", lambda h: h.drain())
        if debug and s == 0:
            em.op("vector", lambda h: h.tensor_copy(snap2[:, :], y_fm[:, :]), inc=vP)
        heads_done = em.now(vP)
        em.wait("tensor", vP, heads_done)
        if SERIALIZE:
            em.barrier(vP, sP, tP)
        if s == 1 and stageB is not None and stageB < 4:
            break
        for t in range(4):
            for grp in range(4):
                pt = pt_a if grp % 2 == 0 else pt_b
                key = "pt_a" if grp % 2 == 0 else "pt_b"
                pe_write(key)
                for hh in range(4):
                    h_ = grp * 4 + hh
                    em.op("tensor", lambda h, t=t, h_=h_, pt=pt, hh=hh: h.transpose(
                        pt[:, 128 * hh:128 * (hh + 1)],
                        y_fm[:, 512 * h_ + 128 * t:512 * h_ + 128 * (t + 1)],
                        ident[:, :]), inc=(tP if hh == 3 else None))
                em.wait("vector", tP, em.now(tP))
                em.op("vector", lambda h, t=t, grp=grp, pt=pt, t0=t0: h.tensor_tensor(
                    out=x_tiles[t0 + t][:, 512 * grp:512 * (grp + 1)],
                    in0=x_tiles[t0 + t][:, 512 * grp:512 * (grp + 1)],
                    in1=pt[:, 0:512], op=ALU.add), inc=vP)
                consumed(key, vP)
        if not do_back:
            break
        add_done = em.now(vP)
        em.wait("sync", vP, add_done)
        for t in range(4):
            em.dma("sync", y_out[128 * (t0 + t):128 * (t0 + t + 1), :],
                   x_tiles[t0 + t][:, :], st)
    if debug:
        em.wait("sync", vP, em.now(vP))
        em.wait("sync", sP, em.now(sP))
        em.wait("sync", tP, em.now(tP))
        em.dma("sync", dbg_xfm, x_fm[:, :], st)
        em.dma("sync", dbg_yfm, y_fm[:, :], st)
        em.dma("sync", dbg_xln, xln[0][:, :], st)
        em.dma("sync", dbg_sn1, snap1[:, :], st)
        em.dma("sync", dbg_sn2, snap2[:, :], st)
        em.dma("sync", dbg_id0, idsnap[:, :], st)
        em.dma("sync", dbg_id1, ident[:, :], st)
        if stage is not None and not do_back:
            for t in range(4):
                em.dma("sync", y_out[128 * t:128 * (t + 1), :], x_tiles[t][:, :], st)
            for t in range(4, NT):
                em.dma("sync", y_out[128 * t:128 * (t + 1), :], x_tiles[t][:, :], st)
    em.wait("sync", st, em.now(st))
    em.emit_and_close()
    es.close()
    return nc


IDENT_BF16 = np.eye(128, dtype=ml_dtypes.bfloat16)


def run_launchA(x_full, debug=False, stage=None, stageB=None):
    shards = shard_rows(np.asarray(x_full, np.float32))
    nc = build_launchA(debug=debug, stage=stage, stageB=stageB)
    in_maps = [{"x": s, "ident_hbm": IDENT_BF16} for s in shards]
    res = run_bass_kernel_spmd(nc, in_maps, core_ids=list(range(NCORES)))
    if debug:
        return unshard_rows([r["y"] for r in res.results]), res
    return unshard_rows([r["y"] for r in res.results])


def kernel(**inputs):
    y1 = run_launchA(inputs["L0_attn_x"])
    gu_all, d_all, sgu, sd = run_launchB1(inputs)
    y2 = run_launchB2(y1, inputs, gu_all, d_all, sgu, sd)
    q_list, k_list, v_list = run_launchC(y2, inputs)
    y3 = run_launchD(y2, inputs, q_list, k_list, v_list)
    qgT, quT, qdT = run_launchE12(inputs)
    y4 = run_launchE3(y3, qgT, quT, qdT)
    return y4



def kappa_orig_rows():
    out = np.empty(2048, np.int64)
    for c in range(2):
        for j in range(8):
            for p in range(128):
                out[1024 * c + 128 * j + p] = 1024 * c + 8 * p + j
    return out


def lam_orig_cols():
    out = np.empty(1536, np.int64)
    for j in range(8):
        for pg in range(192):
            out[192 * j + pg] = 8 * pg + j
    return out


KAPPA_ORIG = kappa_orig_rows()
LAM_ORIG = lam_orig_cols()
GU_COL_PERM = np.concatenate([LAM_ORIG, 1536 + LAM_ORIG])


def emit_dequant_job(em, bufs, w_src, sc_src, dst, W, N, n_chunks, dst_row_of,
                     sc_row_base_of, ld, lg, vP, st, tail_w=None):
    ALUx = ALU
    (wbuf, sc_t, plane, uu, u16t, mlo, v2, vfix, sfac, val, dense) = bufs
    for chunk in range(n_chunks):
        cw = W if (tail_w is None or chunk < n_chunks - 1) else tail_w
        em.dma("sync", wbuf[0:cw, :N], w_src[W * chunk:W * chunk + cw, :], ld)
        base = sc_row_base_of(chunk)
        ngr = (cw + 15) // 16
        src3 = sc_src[base:base + ngr][:, None, :].broadcast_to([ngr, 16, N])
        em.wait("gpsimd", vP, em.now(vP))
        em.dma("gpsimd", sc_t[0:cw, :N], src3, lg)
        em.wait("vector", ld, em.now(ld))
        em.wait("vector", lg, em.now(lg))
        for jp in range(4):
            for j in (2 * jp, 2 * jp + 1):
                b = j % 2
                if (chunk, jp) != (0, 0) or j != 2 * jp:
                    pass
            j0, j1 = 2 * jp, 2 * jp + 1
            em.wait("vector", st, em.now(st))
            for (dst_t, sh) in ((0, j0 * 4), (1, j1 * 4)):
                em.op("vector", lambda h, dst_t=dst_t, sh=sh, cw=cw: h.tensor_scalar(
                    out=plane[dst_t][0:cw, :N], in0=wbuf[0:cw, :N], scalar1=sh,
                    scalar2=15, op0=ALUx.logical_shift_right, op1=ALUx.bitwise_and)
                    if sh else h.tensor_scalar(
                    out=plane[dst_t][0:cw, :N], in0=wbuf[0:cw, :N], scalar1=15,
                    scalar2=None, op0=ALUx.bitwise_and))
            em.op("vector", lambda h: h.drain())
            for b in range(2):
                em.op("vector", lambda h, b=b, cw=cw: h.tensor_scalar(
                    out=uu[b][0:cw, :N], in0=plane[b][0:cw, :N], scalar1=7,
                    scalar2=None, op0=ALUx.bitwise_and))
            em.op("vector", lambda h: h.drain())
            for b in range(2):
                em.op("vector", lambda h, b=b, cw=cw: h.tensor_scalar(
                    out=u16t[b][0:cw, :N], in0=uu[b][0:cw, :N], scalar1=64,
                    scalar2=0x3F00, op0=ALUx.mult, op1=ALUx.add))
            em.op("vector", lambda h: h.drain())
            for b in range(2):
                em.op("vector", lambda h, b=b, cw=cw: h.tensor_scalar(
                    out=mlo[b][0:cw, :N], in0=uu[b][0:cw, :N], scalar1=2,
                    scalar2=None, op0=ALUx.is_lt))
            em.op("vector", lambda h: h.drain())
            for b in range(2):
                em.op("vector", lambda h, b=b, cw=cw: h.tensor_scalar(
                    out=v2[b][0:cw, :N], in0=u16t[b][0:cw, :N].bitcast(BF16),
                    scalar1=2.0, scalar2=1.0, op0=ALUx.mult, op1=ALUx.subtract))
            em.op("vector", lambda h: h.drain())
            for b in range(2):
                em.op("vector", lambda h, b=b, cw=cw: h.select(
                    vfix[b][0:cw, :N], mlo[b][0:cw, :N], v2[b][0:cw, :N],
                    u16t[b][0:cw, :N].bitcast(BF16), add_drain=True))
            em.op("vector", lambda h: h.drain())
            for b in range(2):
                em.op("vector", lambda h, b=b, cw=cw: h.tensor_scalar(
                    out=sfac[b][0:cw, :N], in0=plane[b][0:cw, :N], scalar1=8,
                    scalar2=None, op0=ALUx.is_ge))
            em.op("vector", lambda h: h.drain())
            for b in range(2):
                em.op("vector", lambda h, b=b, cw=cw: h.tensor_scalar(
                    out=sfac[b][0:cw, :N], in0=sfac[b][0:cw, :N], scalar1=-2.0,
                    scalar2=1.0, op0=ALUx.mult, op1=ALUx.add))
            em.op("vector", lambda h: h.drain())
            for b in range(2):
                em.op("vector", lambda h, b=b, cw=cw: h.tensor_tensor(
                    out=val[b][0:cw, :N], in0=sfac[b][0:cw, :N],
                    in1=vfix[b][0:cw, :N], op=ALUx.mult))
            em.op("vector", lambda h: h.drain())
            for b in range(2):
                em.op("vector", lambda h, b=b, cw=cw: h.tensor_tensor(
                    out=dense[b][0:cw, :N], in0=val[b][0:cw, :N],
                    in1=sc_t[0:cw, :N], op=ALUx.mult),
                    inc=(vP if b == 1 else None))
            em.op("vector", lambda h: h.drain())
            em.wait("sync", vP, em.now(vP))
            for b, j in ((0, j0), (1, j1)):
                r0 = dst_row_of(chunk, j)
                em.dma("sync", dst[r0:r0 + cw, :], dense[b][0:cw, :N], st)
        em.op("vector", lambda h: h.drain())
        em.wait("sync", vP, em.now(vP))


def build_launchB1():
    nc = bass.Bass("TRN2", target_bir_lowering=False)
    gu_pk = nc.dram_tensor("gu_pk", [2, 256, 3072], I32, kind="ExternalInput").ap()
    gu_sc = nc.dram_tensor("gu_sc", [2, 16, 3072], F32, kind="ExternalInput").ap()
    d_pk = nc.dram_tensor("d_pk", [2, 192, 2048], I32, kind="ExternalInput").ap()
    d_sc = nc.dram_tensor("d_sc", [2, 12, 2048], F32, kind="ExternalInput").ap()
    sgu_pk = nc.dram_tensor("sgu_pk", [256, 384], I32, kind="ExternalInput").ap()
    sgu_sc = nc.dram_tensor("sgu_sc", [16, 384], F32, kind="ExternalInput").ap()
    sd_pk = nc.dram_tensor("sd_pk", [192, 256], I32, kind="ExternalInput").ap()
    sd_sc = nc.dram_tensor("sd_sc", [12, 256], F32, kind="ExternalInput").ap()
    gu_out = nc.dram_tensor("gu_out", [2, 2048, 3072], BF16, kind="ExternalOutput").ap()
    d_out = nc.dram_tensor("d_out", [2, 1536, 2048], BF16, kind="ExternalOutput").ap()
    sgu_out = nc.dram_tensor("sgu_out", [2048, 384], BF16, kind="ExternalOutput").ap()
    sd_out = nc.dram_tensor("sd_out", [1536, 256], BF16, kind="ExternalOutput").ap()

    es = contextlib.ExitStack()
    sb = lambda nm, shp, dt: es.enter_context(nc.sbuf_tensor(nm, shp, dt))
    NMAX = 3072
    wbuf = sb("wbuf", [128, NMAX], I32)
    sc_t = sb("sc_t", [128, NMAX], BF16)
    plane = [sb(f"plane{b}", [128, NMAX], I32) for b in range(2)]
    uu = [sb(f"uu{b}", [128, NMAX], I32) for b in range(2)]
    u16t = [sb(f"u16t{b}", [128, NMAX], U16) for b in range(2)]
    mlo = [sb(f"mlo{b}", [128, NMAX], U16) for b in range(2)]
    v2 = [sb(f"v2{b}", [128, NMAX], BF16) for b in range(2)]
    vfix = [sb(f"vfix{b}", [128, NMAX], BF16) for b in range(2)]
    sfac = [sb(f"sfac{b}", [128, NMAX], BF16) for b in range(2)]
    val = [sb(f"val{b}", [128, NMAX], BF16) for b in range(2)]
    dense = [sb(f"dense{b}", [128, NMAX], BF16) for b in range(2)]
    bufs = (wbuf, sc_t, plane, uu, u16t, mlo, v2, vfix, sfac, val, dense)

    em = Emit(nc)
    ld = em.sem("ld")
    lg = em.sem("lg")
    vP = em.sem("vP")
    st = em.sem("st")

    for e in range(2):
        emit_dequant_job(em, bufs, gu_pk[e], gu_sc[e], gu_out[e], 128, 3072, 2,
                         lambda c, j: 1024 * c + 128 * j, lambda c: 8 * c,
                         ld, lg, vP, st)
        emit_dequant_job(em, bufs, d_pk[e], d_sc[e], d_out[e], 128, 2048, 2,
                         lambda c, j: 192 * j + 128 * c, lambda c: 8 * c,
                         ld, lg, vP, st, tail_w=64)
    emit_dequant_job(em, bufs, sgu_pk, sgu_sc, sgu_out, 128, 384, 2,
                     lambda c, j: 1024 * c + 128 * j, lambda c: 8 * c,
                     ld, lg, vP, st)
    emit_dequant_job(em, bufs, sd_pk, sd_sc, sd_out, 128, 256, 2,
                     lambda c, j: 192 * j + 128 * c, lambda c: 8 * c,
                     ld, lg, vP, st, tail_w=64)
    em.wait("sync", st, em.now(st))
    em.emit_and_close()
    es.close()
    return nc


def prep_moe_weight_inputs(inputs, core):
    gperm = GU_COL_PERM
    gu_pk = np.ascontiguousarray(inputs["L0_ffn_gate_up_packed"][2 * core:2 * core + 2][:, :, gperm])
    gu_sc = np.ascontiguousarray(inputs["L0_ffn_gate_up_scales"][2 * core:2 * core + 2][:, :, gperm])
    d_pk = np.ascontiguousarray(inputs["L0_ffn_down_packed"][2 * core:2 * core + 2])
    d_sc = np.ascontiguousarray(inputs["L0_ffn_down_scales"][2 * core:2 * core + 2])
    sgu_pk_all = inputs["L0_ffn_shared_gate_up_packed"][:, gperm]
    sgu_sc_all = inputs["L0_ffn_shared_gate_up_scales"][:, gperm]
    lo, hi = 192 * core, 192 * (core + 1)
    cols = np.r_[lo:hi, 1536 + lo:1536 + hi]
    sgu_pk = np.ascontiguousarray(sgu_pk_all[:, cols])
    sgu_sc = np.ascontiguousarray(sgu_sc_all[:, cols])
    sd_pk = np.ascontiguousarray(inputs["L0_ffn_shared_down_packed"][:, 256 * core:256 * (core + 1)])
    sd_sc = np.ascontiguousarray(inputs["L0_ffn_shared_down_scales"][:, 256 * core:256 * (core + 1)])
    return {"gu_pk": gu_pk, "gu_sc": gu_sc, "d_pk": d_pk, "d_sc": d_sc,
            "sgu_pk": sgu_pk, "sgu_sc": sgu_sc, "sd_pk": sd_pk, "sd_sc": sd_sc}


def run_launchB1(inputs):
    nc = build_launchB1()
    in_maps = [prep_moe_weight_inputs(inputs, c) for c in range(NCORES)]
    res = run_bass_kernel_spmd(nc, in_maps, core_ids=list(range(NCORES)))
    gu_all = np.stack([np.asarray(r["gu_out"]) for r in res.results]).reshape(16, 2048, 3072)
    d_all = np.stack([np.asarray(r["d_out"]) for r in res.results]).reshape(16, 1536, 2048)
    sgu_gate = np.concatenate([np.asarray(r["sgu_out"])[:, :192] for r in res.results], axis=1)
    sgu_up = np.concatenate([np.asarray(r["sgu_out"])[:, 192:] for r in res.results], axis=1)
    sgu = np.concatenate([sgu_gate, sgu_up], axis=1)
    sd = np.concatenate([np.asarray(r["sd_out"]) for r in res.results], axis=1)
    return gu_all, d_all, sgu, sd



def build_launchB2():
    nc = bass.Bass("TRN2", target_bir_lowering=False)
    y_in = nc.dram_tensor("y1", [TPC, D], F32, kind="ExternalInput").ap()
    gwT_in = nc.dram_tensor("gwT", [D, 16], F32, kind="ExternalInput").ap()
    ident_in = nc.dram_tensor("ident_hbm", [128, 128], BF16, kind="ExternalInput").ap()
    gu_all = nc.dram_tensor("gu_all", [16, 2048, 3072], BF16, kind="ExternalInput").ap()
    d_all = nc.dram_tensor("d_all", [16, 1536, 2048], BF16, kind="ExternalInput").ap()
    sgu = nc.dram_tensor("sgu", [2048, 3072], BF16, kind="ExternalInput").ap()
    sd = nc.dram_tensor("sd", [1536, 2048], BF16, kind="ExternalInput").ap()
    y_out = nc.dram_tensor("y2", [TPC, D], F32, kind="ExternalOutput").ap()

    es = contextlib.ExitStack()
    sb = lambda nm, shp, dt: es.enter_context(nc.sbuf_tensor(nm, shp, dt))
    ps = lambda nm, shp, dt: es.enter_context(nc.psum_tensor(nm, shp, dt))

    x_tiles = [sb(f"x{i}", [128, D], F32) for i in range(NT)]
    xln = [sb(f"xln{i}", [128, D], BF16) for i in range(NT)]
    stats = sb("stats", [128, 64], F32)
    ident = sb("ident", [128, 128], BF16)
    gw_f = sb("gw_f", [128, 16 * 16], F32)
    gw_b = sb("gw_b", [128, 16 * 16], BF16)
    x_fm = sb("x_fm", [128, 16 * TPC], BF16)
    comb = sb("comb", [128, NT * 16], F32)
    lg_sb = sb("lg_sb", [128, 16], F32)
    m8 = sb("m8", [128, 8], F32)
    gsc = sb("gsc", [128, 8], F32)
    eun = sb("eun", [128, 16], F32)
    msk = sb("msk", [128, 16], F32)
    pm = sb("pm", [128, 16], F32)
    gu_sb = sb("gu_sb", [128, 16 * 1536], BF16)
    dsb = sb("dsb", [128, 6 * 2048], BF16)
    g_act = sb("g_act", [128, 384], BF16)
    a_tm = sb("a_tm", [128, 768], BF16)
    a_fm = sb("a_fm", [128, 768], BF16)

    p_gate = ps("p_gate", [128, 512], F32)
    p_g = ps("p_g", [128, 512], F32)
    p_u = ps("p_u", [128, 512], F32)
    p_y = [ps(f"p_y{n}", [128, 512], F32) for n in range(4)]
    p_tr = ps("p_tr", [128, 1024], BF16)

    em = Emit(nc)
    ld = em.sem("ld")
    vP = em.sem("vP")
    sP = em.sem("sP")
    tP = em.sem("tP")
    st = em.sem("st")

    guard = {}

    def pe_write(bank):
        if bank in guard:
            s_, v_ = guard[bank]
            em.wait("tensor", s_, v_)

    def consumed(bank, sem):
        guard[bank] = (sem, em.now(sem))

    for i in range(NT):
        em.dma("sync", x_tiles[i][:, :], y_in[128 * i:128 * (i + 1), :], ld)
    em.dma("sync", ident[:, :], ident_in, ld)
    for k in range(16):
        em.dma("sync", gw_f[:, 16 * k:16 * (k + 1)], gwT_in[128 * k:128 * (k + 1), :], ld)
    ld_all = em.now(ld)
    em.wait("vector", ld, ld_all)
    em.wait("scalar", ld, ld_all)
    em.op("vector", lambda h: h.tensor_copy(gw_b[:, :], gw_f[:, :]), inc=vP)

    ln_done = emit_ln(em, x_tiles, xln, stats, None, list(range(NT)), vP, sP)

    em.wait("tensor", ld, ld_all)
    em.wait("tensor", vP, ln_done)
    for k in range(16):
        c, j = divmod(k, 8)
        for t in range(NT):
            pe_write("p_tr")
            em.op("tensor", lambda h, k=k, c=c, j=j, t=t: h.transpose(
                p_tr[:, 0:128], xln[t][:, 1024 * c + j:1024 * (c + 1):8],
                ident[:, :]), inc=tP)
            em.wait("vector", tP, em.now(tP))
            em.op("vector", lambda h, k=k, t=t: h.tensor_copy(
                x_fm[:, 1024 * k + 128 * t:1024 * k + 128 * (t + 1)], p_tr[:, 0:128]),
                inc=vP)
            consumed("p_tr", vP)
    fm_ready = em.now(vP)
    em.wait("tensor", vP, fm_ready)

    for t in range(NT):
        pe_write("p_gate")
        for k in range(16):
            em.op("tensor", lambda h, k=k, t=t: h.matmul(
                p_gate[:, 0:16], x_fm[:, 1024 * k + 128 * t:1024 * k + 128 * (t + 1)],
                gw_b[:, 16 * k:16 * (k + 1)], start=(k == 0), stop=(k == 15)),
                inc=(tP if k == 15 else None))
        em.wait("vector", tP, em.now(tP))
        em.op("vector", lambda h: h.tensor_copy(lg_sb[:, :], p_gate[:, 0:16]))
        em.op("vector", lambda h: h.drain())
        em.op("vector", lambda h: h.max(out=m8[:, :], in_=lg_sb[:, :]))
        em.op("vector", lambda h: h.drain())
        em.op("vector", lambda h: h.tensor_scalar(
            out=gsc[:, 0:1], in0=m8[:, 0:1], scalar1=-1.0, scalar2=None,
            op0=ALU.mult), inc=vP)
        em.wait("scalar", vP, em.now(vP))
        em.op("scalar", lambda h: h.activation(
            out=eun[:, :], in_=lg_sb[:, :], func=AF.Exp, bias=gsc[:, 0:1],
            scale=1.0), inc=sP)
        consumed("p_gate", sP)
        em.wait("vector", sP, em.now(sP))
        em.op("vector", lambda h: h.tensor_scalar(
            out=msk[:, :], in0=lg_sb[:, :], scalar1=m8[:, 3:4], scalar2=None,
            op0=ALU.is_ge))
        em.op("vector", lambda h: h.drain())
        em.op("vector", lambda h: h.tensor_tensor(
            out=pm[:, :], in0=eun[:, :], in1=msk[:, :], op=ALU.mult))
        em.op("vector", lambda h: h.drain())
        em.op("vector", lambda h: h.tensor_reduce(
            out=gsc[:, 1:2], in_=pm[:, :], axis=AX.X, op=ALU.add))
        em.op("vector", lambda h: h.drain())
        em.op("vector", lambda h: h.reciprocal(gsc[:, 2:3], gsc[:, 1:2]))
        em.op("vector", lambda h: h.drain())
        em.op("vector", lambda h, t=t: h.tensor_scalar(
            out=comb[:, 16 * t:16 * (t + 1)], in0=pm[:, :], scalar1=gsc[:, 2:3],
            scalar2=None, op0=ALU.mult), inc=vP)
        em.op("vector", lambda h: h.drain())
    gate_done = em.now(vP)

    em.wait("sync", ld, ld_all)
    for e in range(17):
        shared = (e == 16)
        for ih in range(2):
            em.wait("sync", tP, em.now(tP))
            for k in range(16):
                for sec in range(2):
                    srcc = (sgu if shared else gu_all[e])
                    em.dma("sync",
                           gu_sb[:, 1536 * k + 768 * sec:1536 * k + 768 * (sec + 1)],
                           srcc[128 * k:128 * (k + 1),
                                1536 * sec + 768 * ih:1536 * sec + 768 * (ih + 1)],
                           ld)
            for k2 in range(6):
                srcd = (sd if shared else d_all[e])
                em.dma("sync", dsb[:, 2048 * k2:2048 * (k2 + 1)],
                       srcd[768 * ih + 128 * k2:768 * ih + 128 * (k2 + 1), :], ld)
            w_ready = em.now(ld)
            em.wait("tensor", ld, w_ready)
            for t in range(NT):
                for p in range(2):
                    cb = 384 * p
                    pe_write("p_g")
                    pe_write("p_u")
                    for k in range(16):
                        em.op("tensor", lambda h, k=k, t=t, cb=cb: h.matmul(
                            p_g[:, 0:384],
                            x_fm[:, 1024 * k + 128 * t:1024 * k + 128 * (t + 1)],
                            gu_sb[:, 1536 * k + cb:1536 * k + cb + 384],
                            start=(k == 0), stop=(k == 15)))
                        em.op("tensor", lambda h, k=k, t=t, cb=cb: h.matmul(
                            p_u[:, 0:384],
                            x_fm[:, 1024 * k + 128 * t:1024 * k + 128 * (t + 1)],
                            gu_sb[:, 1536 * k + 768 + cb:1536 * k + 768 + cb + 384],
                            start=(k == 0), stop=(k == 15)),
                            inc=(tP if k == 15 else None))
                    em.wait("scalar", tP, em.now(tP))
                    em.wait("scalar", vP, em.now(vP))
                    em.op("scalar", lambda h: h.activation(
                        out=g_act[:, :], in_=p_g[:, 0:384], func=AF.Silu), inc=sP)
                    consumed("p_g", sP)
                    em.wait("vector", sP, em.now(sP))
                    em.op("vector", lambda h, p=p: h.tensor_tensor(
                        out=a_tm[:, 384 * p:384 * (p + 1)], in0=p_u[:, 0:384],
                        in1=g_act[:, :], op=ALU.mult), inc=vP)
                    consumed("p_u", vP)
                a_done = em.now(vP)
                em.wait("tensor", vP, a_done)
                for k2 in range(6):
                    pe_write("p_tr")
                    em.op("tensor", lambda h, k2=k2: h.transpose(
                        p_tr[:, 0:128], a_tm[:, 128 * k2:128 * (k2 + 1)],
                        ident[:, :]), inc=tP)
                    em.wait("vector", tP, em.now(tP))
                    em.op("vector", lambda h, k2=k2: h.tensor_copy(
                        a_fm[:, 128 * k2:128 * (k2 + 1)], p_tr[:, 0:128]), inc=vP)
                    consumed("p_tr", vP)
                em.wait("tensor", vP, em.now(vP))
                for n in range(4):
                    pe_write(f"p_y{n}")
                for k2 in range(6):
                    for n in range(4):
                        em.op("tensor", lambda h, k2=k2, n=n: h.matmul(
                            p_y[n][:, 0:512], a_fm[:, 128 * k2:128 * (k2 + 1)],
                            dsb[:, 2048 * k2 + 512 * n:2048 * k2 + 512 * (n + 1)],
                            start=(k2 == 0), stop=(k2 == 5)),
                            inc=(tP if (k2 == 5 and n == 3) else None))
                em.wait("vector", tP, em.now(tP))
                for n in range(4):
                    if shared:
                        em.op("vector", lambda h, t=t, n=n: h.tensor_tensor(
                            out=x_tiles[t][:, 512 * n:512 * (n + 1)],
                            in0=x_tiles[t][:, 512 * n:512 * (n + 1)],
                            in1=p_y[n][:, 0:512], op=ALU.add),
                            inc=(vP if n == 3 else None))
                    else:
                        em.op("vector", lambda h, t=t, n=n, e=e: h.scalar_tensor_tensor(
                            out=x_tiles[t][:, 512 * n:512 * (n + 1)],
                            in0=p_y[n][:, 0:512], scalar=comb[:, 16 * t + e:16 * t + e + 1],
                            in1=x_tiles[t][:, 512 * n:512 * (n + 1)],
                            op0=ALU.mult, op1=ALU.add),
                            inc=(vP if n == 3 else None))
                em.op("vector", lambda h: h.drain())
                for n in range(4):
                    consumed(f"p_y{n}", vP)
    done = em.now(vP)
    em.wait("sync", vP, done)
    for t in range(NT):
        em.dma("sync", y_out[128 * t:128 * (t + 1), :], x_tiles[t][:, :], st)
    em.wait("sync", st, em.now(st))
    em.emit_and_close()
    es.close()
    return nc


def gate_wT_perm(gate_w):
    return np.ascontiguousarray(gate_w.T[KAPPA_ORIG])


def run_launchB2(y1_full, inputs, gu_all, d_all, sgu, sd):
    shards = shard_rows(np.asarray(y1_full, np.float32))
    gwT = gate_wT_perm(np.asarray(inputs["L0_ffn_gate_w"], np.float32))
    nc = build_launchB2()
    in_maps = [{"y1": s, "gwT": gwT, "ident_hbm": IDENT_BF16,
                "gu_all": gu_all, "d_all": d_all, "sgu": sgu, "sd": sd}
               for s in shards]
    res = run_bass_kernel_spmd(nc, in_maps, core_ids=list(range(NCORES)))
    return unshard_rows([np.asarray(r["y2"]) for r in res.results])



def build_launchC():
    nc = bass.Bass("TRN2", target_bir_lowering=False)
    y_in = nc.dram_tensor("y2", [TPC, D], F32, kind="ExternalInput").ap()
    w_in = {}
    b_in = {}
    for w in ("q", "k", "v"):
        w_in[w] = nc.dram_tensor(f"W{w}", [D, D], F32, kind="ExternalInput").ap()
        b_in[w] = nc.dram_tensor(f"b{w}", [1, D], F32, kind="ExternalInput").ap()
    ident_in = nc.dram_tensor("ident_hbm", [128, 128], BF16, kind="ExternalInput").ap()
    q_out = nc.dram_tensor("q_fm", [D, TPC], BF16, kind="ExternalOutput").ap()
    k_out = nc.dram_tensor("k_fm", [D, TPC], BF16, kind="ExternalOutput").ap()
    v_out = nc.dram_tensor("v_tm", [TPC, D], BF16, kind="ExternalOutput").ap()

    es = contextlib.ExitStack()
    sb = lambda nm, shp, dt: es.enter_context(nc.sbuf_tensor(nm, shp, dt))
    ps = lambda nm, shp, dt: es.enter_context(nc.psum_tensor(nm, shp, dt))

    x_tiles = [sb(f"x{i}", [128, D], F32) for i in range(NT)]
    xln = [sb(f"xln{i}", [128, D], BF16) for i in range(NT)]
    stats = sb("stats", [128, 64], F32)
    ident = sb("ident", [128, 128], BF16)
    x_fm = sb("x_fm", [128, 16 * TPC], BF16)
    wsb = sb("wsb", [128, 16 * D], BF16)
    bias = sb("bias", [1, D], BF16)
    ones1 = sb("ones1", [1, 128], BF16)
    oneT = sb("oneT", [1, 512], BF16)
    outb = sb("outb", [128, D], BF16)
    o_fm = sb("o_fm", [128, 512], BF16)
    p_tr = ps("p_tr", [128, 1024], BF16)
    p_o = [ps(f"p_o{i}", [128, 512], F32) for i in range(2)]

    em = Emit(nc)
    ld = em.sem("ld")
    lg = em.sem("lg")
    vP = em.sem("vP")
    sP = em.sem("sP")
    tP = em.sem("tP")
    st = em.sem("st")

    guard = {}

    def pe_write(bank):
        if bank in guard:
            s_, v_ = guard[bank]
            em.wait("tensor", s_, v_)

    def consumed(bank, sem):
        guard[bank] = (sem, em.now(sem))

    for i in range(NT):
        em.dma("sync", x_tiles[i][:, :], y_in[128 * i:128 * (i + 1), :], ld)
    em.dma("sync", ident[:, :], ident_in, ld)
    ld_all = em.now(ld)
    em.wait("vector", ld, ld_all)
    em.wait("scalar", ld, ld_all)
    em.op("vector", lambda h: h.memset(ones1[:, :], 1.0))
    em.op("vector", lambda h: h.memset(oneT[:, :], 1.0), inc=vP)
    ln_done = emit_ln(em, x_tiles, xln, stats, None, list(range(NT)), vP, sP)
    em.wait("tensor", ld, ld_all)
    em.wait("tensor", vP, ln_done)
    for k in range(16):
        c, j = divmod(k, 8)
        for t in range(NT):
            pe_write("p_tr")
            em.op("tensor", lambda h, c=c, j=j, t=t: h.transpose(
                p_tr[:, 0:128], xln[t][:, 1024 * c + j:1024 * (c + 1):8],
                ident[:, :]), inc=tP)
            em.wait("vector", tP, em.now(tP))
            em.op("vector", lambda h, k=k, t=t: h.tensor_copy(
                x_fm[:, 1024 * k + 128 * t:1024 * k + 128 * (t + 1)], p_tr[:, 0:128]),
                inc=vP)
            consumed("p_tr", vP)
    fm_ready = em.now(vP)
    em.wait("tensor", vP, fm_ready)

    for wi, wname in enumerate(("q", "k", "v")):
        em.wait("gpsimd", tP, em.now(tP))
        for k in range(16):
            em.dma("gpsimd", wsb[:, D * k:D * (k + 1)],
                   w_in[wname][128 * k:128 * (k + 1), :], lg)
        em.dma("gpsimd", bias[:, :], b_in[wname], lg)
        w_ready = em.now(lg)
        em.wait("tensor", lg, w_ready)
        if wname in ("q", "k"):
            dst = q_out if wname == "q" else k_out
            for m in range(16):
                for th in range(2):
                    bank = th % 2
                    pe_write(f"p_o{bank}")
                    for k in range(16):
                        em.op("tensor", lambda h, m=m, th=th, k=k, bank=bank: h.matmul(
                            p_o[bank][:, :],
                            wsb[:, D * k + 128 * m:D * k + 128 * (m + 1)],
                            x_fm[:, 1024 * k + 512 * th:1024 * k + 512 * (th + 1)],
                            start=(k == 0), stop=False))
                    em.op("tensor", lambda h, m=m, th=th, bank=bank: h.matmul(
                        p_o[bank][:, :],
                        bias[:, 128 * m:128 * (m + 1)], oneT[:, :],
                        start=False, stop=True), inc=tP)
                    em.wait("vector", tP, em.now(tP))
                    em.op("vector", lambda h, bank=bank: h.tensor_copy(
                        o_fm[:, :], p_o[bank][:, :]), inc=vP)
                    consumed(f"p_o{bank}", vP)
                    em.wait("sync", vP, em.now(vP))
                    em.dma("sync", dst[128 * m:128 * (m + 1), 512 * th:512 * (th + 1)],
                           o_fm[:, :], st)
                    em.wait("vector", st, em.now(st))
        else:
            for t in range(NT):
                for n in range(4):
                    bank = n % 2
                    pe_write(f"p_o{bank}")
                    for k in range(16):
                        em.op("tensor", lambda h, t=t, n=n, k=k, bank=bank: h.matmul(
                            p_o[bank][:, :],
                            x_fm[:, 1024 * k + 128 * t:1024 * k + 128 * (t + 1)],
                            wsb[:, D * k + 512 * n:D * k + 512 * (n + 1)],
                            start=(k == 0), stop=False))
                    em.op("tensor", lambda h, t=t, n=n, bank=bank: h.matmul(
                        p_o[bank][:, :], ones1[:, 0:128],
                        bias[:, 512 * n:512 * (n + 1)],
                        start=False, stop=True), inc=tP)
                    em.wait("vector", tP, em.now(tP))
                    em.op("vector", lambda h, bank=bank: h.tensor_copy(
                        o_fm[:, :], p_o[bank][:, :]), inc=vP)
                    consumed(f"p_o{bank}", vP)
                    em.wait("sync", vP, em.now(vP))
                    em.dma("sync", v_out[128 * t:128 * (t + 1), 512 * n:512 * (n + 1)],
                           o_fm[:, :], st)
                    em.wait("vector", st, em.now(st))
    em.wait("sync", st, em.now(st))
    em.emit_and_close()
    es.close()
    return nc


def run_launchC(y2_full, inputs):
    shards = shard_rows(np.asarray(y2_full, np.float32))
    nc = build_launchC()
    base = {"ident_hbm": IDENT_BF16}
    for w, nm in (("q", "Wq"), ("k", "Wk"), ("v", "Wv")):
        base[f"W{w}"] = np.ascontiguousarray(
            np.asarray(inputs[f"L1_attn_{nm}"], np.float32).T[KAPPA_ORIG])
        base[f"b{w}"] = np.asarray(inputs[f"L1_attn_b{w}"], np.float32).reshape(1, D)
    in_maps = [dict(base, y2=s) for s in shards]
    res = run_bass_kernel_spmd(nc, in_maps, core_ids=list(range(NCORES)))
    return ([np.asarray(r["q_fm"]) for r in res.results],
            [np.asarray(r["k_fm"]) for r in res.results],
            [np.asarray(r["v_tm"]) for r in res.results])



ALIBI_SLOPES = None


def _alibi_slopes(n):
    def pow2_slopes(n):
        start = 2 ** (-2 ** (-(math.log2(n) - 3)))
        return [start * start ** i for i in range(n)]
    if math.log2(n).is_integer():
        return pow2_slopes(n)
    c = 2 ** math.floor(math.log2(n))
    return pow2_slopes(c) + _alibi_slopes(2 * c)[::2][:n - c]


def build_launchD(core):
    sA, sB = strip_list(core)[0][1], strip_list(core)[1][1]
    nc = bass.Bass("TRN2", target_bir_lowering=False)
    y_in = nc.dram_tensor("y2", [TPC, D], F32, kind="ExternalInput").ap()
    q_in = nc.dram_tensor("q_fm", [D, TPC], BF16, kind="ExternalInput").ap()
    k_in = nc.dram_tensor("k_glob", [D, S], BF16, kind="ExternalInput").ap()
    v_in = nc.dram_tensor("v_hm", [16, S, 128], BF16, kind="ExternalInput").ap()
    wo_in = nc.dram_tensor("WoT", [D, D], F32, kind="ExternalInput").ap()
    bo_in = nc.dram_tensor("bo", [1, D], F32, kind="ExternalInput").ap()
    ident_in = nc.dram_tensor("ident_hbm", [128, 128], BF16, kind="ExternalInput").ap()
    y_out = nc.dram_tensor("y3", [TPC, D], F32, kind="ExternalOutput").ap()

    es = contextlib.ExitStack()
    sb = lambda nm, shp, dt: es.enter_context(nc.sbuf_tensor(nm, shp, dt))
    ps = lambda nm, shp, dt: es.enter_context(nc.psum_tensor(nm, shp, dt))

    x_tiles = [sb(f"x{i}", [128, D], F32) for i in range(NT)]
    q_sb = sb("q_sb", [128, 16 * TPC], BF16)
    att_fm = sb("att_fm", [128, 16 * TPC], BF16)
    k_sb = sb("k_sb", [128, S], BF16)
    v_sb = sb("v_sb", [128, S], BF16)
    sc_sb = sb("sc_sb", [128, S], F32)
    p_sb = sb("p_sb", [128, S], BF16)
    pT_sb = sb("pT_sb", [128, S], BF16)
    rel = sb("rel", [128, 512], F32)
    mrel = sb("mrel", [128, 512], U16)
    negt = sb("negt", [128, 512], F32)
    srow = sb("srow", [128, 8], F32)
    wo_sb = sb("wo_sb", [128, 16 * 512], BF16)
    bo_sb = sb("bo_sb", [1, D], BF16)
    ones1 = sb("ones1", [1, 128], BF16)
    ident = sb("ident", [128, 128], BF16)

    p_s = ps("p_s", [128, 512], F32)
    p_tr = ps("p_tr", [128, 1024], BF16)
    p_ov = ps("p_ov", [128, 512], F32)
    p_pr = [ps(f"p_pr{i}", [128, 512], F32) for i in range(2)]

    em = Emit(nc)
    ld = em.sem("ld")
    lg = em.sem("lg")
    gP = em.sem("gP")
    vP = em.sem("vP")
    sP = em.sem("sP")
    tP = em.sem("tP")
    st = em.sem("st")

    guard = {}

    def pe_write(bank):
        if bank in guard:
            s_, v_ = guard[bank]
            em.wait("tensor", s_, v_)

    def consumed(bank, sem):
        guard[bank] = (sem, em.now(sem))

    for i in range(NT):
        em.dma("sync", x_tiles[i][:, :], y_in[128 * i:128 * (i + 1), :], ld)
    for h in range(16):
        em.dma("sync", q_sb[:, TPC * h:TPC * (h + 1)], q_in[128 * h:128 * (h + 1), :], ld)
    em.dma("sync", ident[:, :], ident_in, ld)
    em.wait("vector", ld, em.now(ld))
    em.op("vector", lambda h: h.memset(negt[:, :], -1e30))
    em.op("vector", lambda h: h.memset(ones1[:, :], 1.0), inc=vP)

    slopes = [s * math.sqrt(128.0) for s in _alibi_slopes(16)]
    SC = 128.0 ** -0.5

    em.wait("sync", ld, em.now(ld))
    for si, s in enumerate((sA, sB)):
        L = 512 * (s + 1)
        C = L // 128
        for hh in range(16):
            em.wait("sync", tP, em.now(tP))
            em.dma("sync", k_sb[:, 0:L], k_in[128 * hh:128 * (hh + 1), 0:L], ld)
            for kc in range(C):
                em.dma("sync", v_sb[:, 128 * kc:128 * (kc + 1)],
                       v_in[hh, 128 * kc:128 * (kc + 1), :], ld)
            kv_ready = em.now(ld)
            em.wait("tensor", ld, kv_ready)
            for qt in range(4):
                qloc = 1024 * hh + 512 * si + 128 * qt
                q0 = 512 * s + 128 * qt
                for ks in range(s + 1):
                    em.wait("gpsimd", vP, em.now(vP))
                    em.op("gpsimd", lambda h, q0=q0, ks=ks: h.iota(
                        rel[:, :], pattern=[[-1, 512]], base=q0 - 512 * ks,
                        channel_multiplier=1,
                        allow_small_or_imprecise_dtypes=True), inc=gP)
                    pe_write("p_s")
                    em.op("tensor", lambda h, qloc=qloc, ks=ks: h.matmul(
                        p_s[:, :], q_sb[:, qloc:qloc + 128],
                        k_sb[:, 512 * ks:512 * (ks + 1)],
                        start=True, stop=True), inc=tP)
                    em.wait("vector", tP, em.now(tP))
                    em.wait("vector", gP, em.now(gP))
                    em.op("vector", lambda h, ks=ks, hh=hh: h.scalar_tensor_tensor(
                        out=sc_sb[:, 512 * ks:512 * (ks + 1)], in0=rel[:, :],
                        scalar=-slopes[hh], in1=p_s[:, :],
                        op0=ALU.mult, op1=ALU.add), inc=vP)
                    consumed("p_s", vP)
                    if ks == s:
                        em.op("vector", lambda h: h.drain())
                        em.op("vector", lambda h: h.tensor_scalar(
                            out=mrel[:, :], in0=rel[:, :], scalar1=0.0,
                            scalar2=None, op0=ALU.is_lt))
                        em.op("vector", lambda h: h.drain())
                        em.op("vector", lambda h, ks=ks: h.copy_predicated(
                            sc_sb[:, 512 * ks:512 * (ks + 1)], mrel[:, :],
                            negt[:, :]), inc=vP)
                em.op("vector", lambda h: h.drain())
                em.op("vector", lambda h, L=L: h.tensor_reduce(
                    out=srow[:, 0:1], in_=sc_sb[:, 0:L], axis=AX.X, op=ALU.max))
                em.op("vector", lambda h: h.drain())
                em.op("vector", lambda h: h.tensor_scalar(
                    out=srow[:, 1:2], in0=srow[:, 0:1], scalar1=-SC, scalar2=None,
                    op0=ALU.mult), inc=vP)
                em.wait("scalar", vP, em.now(vP))
                em.op("scalar", lambda h, L=L: h.activation(
                    out=p_sb[:, 0:L], in_=sc_sb[:, 0:L], func=AF.Exp,
                    bias=srow[:, 1:2], scale=SC, accum_out=srow[:, 2:3]), inc=sP)
                em.wait("vector", sP, em.now(sP))
                em.op("vector", lambda h: h.reciprocal(srow[:, 3:4], srow[:, 2:3]))
                em.op("vector", lambda h: h.drain())
                em.op("vector", lambda h, L=L: h.tensor_scalar(
                    out=p_sb[:, 0:L], in0=p_sb[:, 0:L], scalar1=srow[:, 3:4],
                    scalar2=None, op0=ALU.mult), inc=vP)
                pn_done = em.now(vP)
                em.wait("tensor", vP, pn_done)
                for kc in range(C):
                    pe_write("p_tr")
                    em.op("tensor", lambda h, kc=kc: h.transpose(
                        p_tr[:, 0:128], p_sb[:, 128 * kc:128 * (kc + 1)],
                        ident[:, :]), inc=tP)
                    em.wait("vector", tP, em.now(tP))
                    em.op("vector", lambda h, kc=kc: h.tensor_copy(
                        pT_sb[:, 128 * kc:128 * (kc + 1)], p_tr[:, 0:128]), inc=vP)
                    consumed("p_tr", vP)
                em.wait("tensor", vP, em.now(vP))
                pe_write("p_ov")
                for kc in range(C):
                    em.op("tensor", lambda h, kc=kc, C=C: h.matmul(
                        p_ov[:, 0:128], v_sb[:, 128 * kc:128 * (kc + 1)],
                        pT_sb[:, 128 * kc:128 * (kc + 1)],
                        start=(kc == 0), stop=(kc == C - 1)),
                        inc=(tP if kc == C - 1 else None))
                em.wait("vector", tP, em.now(tP))
                em.op("vector", lambda h, qloc=qloc: h.tensor_copy(
                    att_fm[:, qloc:qloc + 128], p_ov[:, 0:128]), inc=vP)
                consumed("p_ov", vP)
    att_done = em.now(vP)
    em.wait("tensor", vP, att_done)

    for n in range(4):
        em.wait("gpsimd", tP, em.now(tP))
        for kd in range(16):
            em.dma("gpsimd", wo_sb[:, 512 * kd:512 * (kd + 1)],
                   wo_in[128 * kd:128 * (kd + 1), 512 * n:512 * (n + 1)], lg)
        if n == 0:
            em.dma("gpsimd", bo_sb[:, :], bo_in, lg)
        em.wait("tensor", lg, em.now(lg))
        for t in range(NT):
            bank = t % 2
            pe_write(f"p_pr{bank}")
            for kd in range(16):
                em.op("tensor", lambda h, t=t, kd=kd, bank=bank: h.matmul(
                    p_pr[bank][:, :], att_fm[:, 1024 * kd + 128 * t:1024 * kd + 128 * (t + 1)],
                    wo_sb[:, 512 * kd:512 * (kd + 1)],
                    start=(kd == 0), stop=False))
            em.op("tensor", lambda h, n=n, bank=bank: h.matmul(
                p_pr[bank][:, :], ones1[:, :], bo_sb[:, 512 * n:512 * (n + 1)],
                start=False, stop=True), inc=tP)
            em.wait("vector", tP, em.now(tP))
            em.op("vector", lambda h, t=t, n=n, bank=bank: h.tensor_tensor(
                out=x_tiles[t][:, 512 * n:512 * (n + 1)],
                in0=x_tiles[t][:, 512 * n:512 * (n + 1)],
                in1=p_pr[bank][:, :], op=ALU.add), inc=vP)
            em.op("vector", lambda h: h.drain())
            consumed(f"p_pr{bank}", vP)
    done = em.now(vP)
    em.wait("sync", vP, done)
    for t in range(NT):
        em.dma("sync", y_out[128 * t:128 * (t + 1), :], x_tiles[t][:, :], st)
    em.wait("sync", st, em.now(st))
    em.emit_and_close()
    es.close()
    return nc


def make_kv_global(k_list, v_list):
    k_glob = [np.empty((D, S), k_list[0].dtype) for _ in range(B)]
    v_glob = [np.empty((S, D), v_list[0].dtype) for _ in range(B)]
    for c in range(NCORES):
        (b1, s1), (b2, s2) = strip_list(c)
        k_glob[b1][:, 512 * s1:512 * (s1 + 1)] = k_list[c][:, 0:512]
        k_glob[b2][:, 512 * s2:512 * (s2 + 1)] = k_list[c][:, 512:1024]
        v_glob[b1][512 * s1:512 * (s1 + 1), :] = v_list[c][0:512, :]
        v_glob[b2][512 * s2:512 * (s2 + 1), :] = v_list[c][512:1024, :]
    v_hm = [np.ascontiguousarray(
        v.reshape(S, 16, 128).transpose(1, 0, 2)) for v in v_glob]
    return k_glob, v_hm


def run_launchD(y2_full, inputs, q_list, k_list, v_list):
    shards = shard_rows(np.asarray(y2_full, np.float32))
    k_glob, v_hm = make_kv_global(k_list, v_list)
    WoT = np.ascontiguousarray(np.asarray(inputs["L1_attn_Wo"], np.float32).T)
    bo = np.asarray(inputs["L1_attn_bo"], np.float32).reshape(1, D)
    outs = [None] * NCORES
    for j in range(4):
        nc = build_launchD(j)
        cores = [j, j + 4]
        in_maps = []
        for c in cores:
            b = c // 4
            in_maps.append({"y2": shards[c], "q_fm": q_list[c],
                            "k_glob": k_glob[b], "v_hm": v_hm[b],
                            "WoT": WoT, "bo": bo, "ident_hbm": IDENT_BF16})
        res = run_bass_kernel_spmd(nc, in_maps, core_ids=cores)
        for i, c in enumerate(cores):
            outs[c] = np.asarray(res.results[i]["y3"])
    return unshard_rows(outs)



def build_launchE1():
    nc = bass.Bass("TRN2", target_bir_lowering=False)
    wg = nc.dram_tensor("wgT_sh", [D, 1024], F32, kind="ExternalInput").ap()
    wu = nc.dram_tensor("wuT_sh", [D, 1024], F32, kind="ExternalInput").ap()
    wd = nc.dram_tensor("wdT_sh", [1024, D], F32, kind="ExternalInput").ap()
    part_out = nc.dram_tensor("part", [1, 3], F32, kind="ExternalOutput").ap()

    es = contextlib.ExitStack()
    sb = lambda nm, shp, dt: es.enter_context(nc.sbuf_tensor(nm, shp, dt))
    ps = lambda nm, shp, dt: es.enter_context(nc.psum_tensor(nm, shp, dt))
    wt = sb("wt", [128, D], F32)
    scr = [sb(f"scr{b}", [128, D], BF16) for b in range(2)]
    acc = sb("acc", [128, 48], F32)
    ones = sb("ones", [128, 1], F32)
    red = sb("red", [1, 48], F32)
    out3 = sb("out3", [1, 3], F32)
    p_r = ps("p_r", [128, 512], F32)

    em = Emit(nc)
    ld = em.sem("ld")
    vP = em.sem("vP")
    sP = em.sem("sP")
    tP = em.sem("tP")
    st = em.sem("st")

    em.op("vector", lambda h: h.memset(ones[:, :], 1.0), inc=vP)
    col = 0
    jobs = [(wg, 16, 1024), (wu, 16, 1024), (wd, 8, D)]
    bounds = []
    for (src_ap, ntiles, width) in jobs:
        c0 = col
        for i in range(ntiles):
            em.wait("sync", sP, em.now(sP))
            em.dma("sync", wt[:, 0:width], src_ap[128 * i:128 * (i + 1), :], ld)
            em.wait("scalar", ld, em.now(ld))
            em.op("scalar", lambda h, col=col, width=width, i=i: h.activation(
                out=scr[i % 2][:, 0:width], in_=wt[:, 0:width], func=AF.Abs,
                accum_out=acc[:, col:col + 1]), inc=sP)
            em.op("scalar", lambda h: h.drain())
            col += 1
        bounds.append((c0, col))
    em.wait("tensor", sP, em.now(sP))
    em.wait("tensor", vP, em.now(vP))
    em.op("tensor", lambda h: h.matmul(p_r[0:1, 0:40], ones[:, :], acc[:, 0:40],
                                       start=True, stop=True), inc=tP)
    em.wait("vector", tP, em.now(tP))
    em.op("vector", lambda h: h.tensor_copy(red[:, 0:40], p_r[0:1, 0:40]))
    em.op("vector", lambda h: h.drain())
    for w, (c0, c1) in enumerate(bounds):
        em.op("vector", lambda h, w=w, c0=c0, c1=c1: h.tensor_reduce(
            out=out3[:, w:w + 1], in_=red[:, c0:c1], axis=AX.X, op=ALU.add))
        em.op("vector", lambda h: h.drain())
    em.op("vector", lambda h: h.tensor_scalar(
        out=out3[:, :], in0=out3[:, :], scalar1=1.0, scalar2=None,
        op0=ALU.mult), inc=vP)
    em.wait("sync", vP, em.now(vP))
    em.dma("sync", part_out, out3[:, :], st)
    em.wait("sync", st, em.now(st))
    em.emit_and_close()
    es.close()
    return nc


def build_launchE2():
    nc = bass.Bass("TRN2", target_bir_lowering=False)
    wg = nc.dram_tensor("wgT_sh", [D, 1024], F32, kind="ExternalInput").ap()
    wu = nc.dram_tensor("wuT_sh", [D, 1024], F32, kind="ExternalInput").ap()
    wd = nc.dram_tensor("wdT_sh", [1024, D], F32, kind="ExternalInput").ap()
    parts = nc.dram_tensor("parts_row", [1, 24], F32, kind="ExternalInput").ap()
    qg = nc.dram_tensor("qgT_sh", [D, 1024], BF16, kind="ExternalOutput").ap()
    qu = nc.dram_tensor("quT_sh", [D, 1024], BF16, kind="ExternalOutput").ap()
    qd = nc.dram_tensor("qdT_sh", [1024, D], BF16, kind="ExternalOutput").ap()

    es = contextlib.ExitStack()
    sb = lambda nm, shp, dt: es.enter_context(nc.sbuf_tensor(nm, shp, dt))
    ps = lambda nm, shp, dt: es.enter_context(nc.psum_tensor(nm, shp, dt))
    wt = sb("wt", [128, D], F32)
    t1 = sb("t1", [128, D], BF16)
    t2 = sb("t2", [128, D], BF16)
    qt = sb("qt", [128, D], BF16)
    prow = sb("prow", [1, 24], F32)
    mrow = sb("mrow", [1, 3], F32)
    ones1 = sb("ones1", [1, 128], F32)
    mcol = sb("mcol", [128, 3], F32)
    scol = sb("scol", [128, 3], F32)
    hcol = sb("hcol", [128, 3], F32)
    bm = sb("bm", [128, 1], F32)
    bp = sb("bp", [128, 1], F32)
    p_t = ps("p_t", [128, 512], F32)

    em = Emit(nc)
    ld = em.sem("ld")
    vP = em.sem("vP")
    sP = em.sem("sP")
    tP = em.sem("tP")
    st = em.sem("st")

    em.dma("sync", prow[:, :], parts, ld)
    em.wait("vector", ld, em.now(ld))
    em.op("vector", lambda h: h.memset(ones1[:, :], 1.0))
    em.op("vector", lambda h: h.memset(bm[:, :], -0.5))
    em.op("vector", lambda h: h.memset(bp[:, :], 0.5))
    for w in range(3):
        em.op("vector", lambda h, w=w: h.tensor_reduce(
            out=mrow[:, w:w + 1], in_=prow[:, 8 * w:8 * (w + 1)], axis=AX.X,
            op=ALU.add))
        em.op("vector", lambda h: h.drain())
    em.op("vector", lambda h: h.tensor_scalar(
        out=mrow[:, :], in0=mrow[:, :], scalar1=1.0 / (8192.0 * D), scalar2=1e-5,
        op0=ALU.mult, op1=ALU.max), inc=vP)
    em.wait("tensor", vP, em.now(vP))
    em.op("tensor", lambda h: h.matmul(
        p_t[:, 0:3], ones1[:, :], mrow[:, :], start=True, stop=True), inc=tP)
    em.wait("vector", tP, em.now(tP))
    em.op("vector", lambda h: h.tensor_copy(mcol[:, :], p_t[:, 0:3]))
    em.op("vector", lambda h: h.drain())
    em.op("vector", lambda h: h.reciprocal(scol[:, :], mcol[:, :]))
    em.op("vector", lambda h: h.drain())
    em.op("vector", lambda h: h.tensor_scalar(
        out=hcol[:, :], in0=mcol[:, :], scalar1=0.5, scalar2=None,
        op0=ALU.mult), inc=vP)
    bcast_done = em.now(vP)
    em.wait("scalar", vP, bcast_done)

    jobs = [(wg, qg, 16, 1024, 0), (wu, qu, 16, 1024, 1), (wd, qd, 8, D, 2)]
    for (src_ap, dst_ap, ntiles, width, w) in jobs:
        for i in range(ntiles):
            em.wait("sync", sP, em.now(sP))
            em.dma("sync", wt[:, 0:width], src_ap[128 * i:128 * (i + 1), :], ld)
            em.wait("scalar", ld, em.now(ld))
            em.op("scalar", lambda h, w=w, width=width: h.activation(
                out=t1[:, 0:width], in_=wt[:, 0:width], func=AF.Sign,
                bias=bm[:, :], scale=scol[:, w:w + 1]))
            em.op("scalar", lambda h: h.drain())
            em.op("scalar", lambda h, w=w, width=width: h.activation(
                out=t2[:, 0:width], in_=wt[:, 0:width], func=AF.Sign,
                bias=bp[:, :], scale=scol[:, w:w + 1]), inc=sP)
            em.wait("vector", sP, em.now(sP))
            em.wait("vector", st, em.now(st))
            em.op("vector", lambda h, width=width: h.tensor_tensor(
                out=qt[:, 0:width], in0=t1[:, 0:width], in1=t2[:, 0:width],
                op=ALU.add))
            em.op("vector", lambda h: h.drain())
            em.op("vector", lambda h, w=w, width=width: h.tensor_scalar(
                out=qt[:, 0:width], in0=qt[:, 0:width], scalar1=hcol[:, w:w + 1],
                scalar2=None, op0=ALU.mult), inc=vP)
            em.wait("sync", vP, em.now(vP))
            em.dma("sync", dst_ap[128 * i:128 * (i + 1), :], qt[:, 0:width], st)
    em.wait("sync", st, em.now(st))
    em.emit_and_close()
    es.close()
    return nc


def run_launchE12(inputs):
    wgT = np.asarray(inputs["L1_ffn_w_gate"], np.float32).T
    wuT = np.asarray(inputs["L1_ffn_w_up"], np.float32).T
    wdT = np.asarray(inputs["L1_ffn_w_down"], np.float32).T
    sh = lambda c: {"wgT_sh": np.ascontiguousarray(wgT[:, 1024 * c:1024 * (c + 1)]),
                    "wuT_sh": np.ascontiguousarray(wuT[:, 1024 * c:1024 * (c + 1)]),
                    "wdT_sh": np.ascontiguousarray(wdT[1024 * c:1024 * (c + 1), :])}
    nc1 = build_launchE1()
    res1 = run_bass_kernel_spmd(nc1, [sh(c) for c in range(NCORES)],
                                core_ids=list(range(NCORES)))
    parts = np.stack([np.asarray(r["part"])[0] for r in res1.results])
    parts_row = np.ascontiguousarray(parts.T.reshape(1, 24))
    nc2 = build_launchE2()
    in_maps = [dict(sh(c), parts_row=parts_row) for c in range(NCORES)]
    res2 = run_bass_kernel_spmd(nc2, in_maps, core_ids=list(range(NCORES)))
    qgT = np.concatenate([np.asarray(r["qgT_sh"]) for r in res2.results], axis=1)
    quT = np.concatenate([np.asarray(r["quT_sh"]) for r in res2.results], axis=1)
    qdT = np.concatenate([np.asarray(r["qdT_sh"]) for r in res2.results], axis=0)
    return qgT, quT, qdT



def build_launchE3():
    nc = bass.Bass("TRN2", target_bir_lowering=False)
    y_in = nc.dram_tensor("y3", [TPC, D], F32, kind="ExternalInput").ap()
    qg_in = nc.dram_tensor("qgT", [D, 8192], BF16, kind="ExternalInput").ap()
    qu_in = nc.dram_tensor("quT", [D, 8192], BF16, kind="ExternalInput").ap()
    qd_in = nc.dram_tensor("qdT", [8192, D], BF16, kind="ExternalInput").ap()
    ident_in = nc.dram_tensor("ident_hbm", [128, 128], BF16, kind="ExternalInput").ap()
    y_out = nc.dram_tensor("y4", [TPC, D], F32, kind="ExternalOutput").ap()

    es = contextlib.ExitStack()
    sb = lambda nm, shp, dt: es.enter_context(nc.sbuf_tensor(nm, shp, dt))
    ps = lambda nm, shp, dt: es.enter_context(nc.psum_tensor(nm, shp, dt))

    x_tiles = [sb(f"x{i}", [128, D], F32) for i in range(NT)]
    h_tg = [sb(f"h{i}", [128, 8192], BF16) for i in range(4)]
    xq_fm = sb("xq_fm", [128, 16 * TPC], BF16)
    ident = sb("ident", [128, 128], BF16)
    xn = sb("xn", [128, D], BF16)
    xn2 = sb("xn2", [128, D], BF16)
    zb = sb("zb", [128, 512], F32)
    zi = sb("zi", [128, 512], I32)
    stats = sb("stats", [128, 32], F32)
    gact = sb("gact", [128, 512], BF16)
    wg_sb = sb("wg_sb", [128, 16 * 512], BF16)
    wu_sb = sb("wu_sb", [128, 16 * 512], BF16)
    wdb = [sb(f"wdb{b}", [128, 512], BF16) for b in range(2)]
    acc16 = sb("acc16", [128, 16], F32)

    p_g = ps("p_g", [128, 512], F32)
    p_u = ps("p_u", [128, 512], F32)
    p_tr = ps("p_tr", [128, 1024], BF16)
    p_y = [ps(f"p_y{n}", [128, 512], F32) for n in range(4)]

    em = Emit(nc)
    ld = em.sem("ld")
    lw = [em.sem("lw0"), em.sem("lw1")]
    vP = em.sem("vP")
    sP = em.sem("sP")
    tP = em.sem("tP")
    st = em.sem("st")

    guard = {}

    def pe_write(bank):
        if bank in guard:
            s_, v_ = guard[bank]
            em.wait("tensor", s_, v_)

    def consumed(bank, sem):
        guard[bank] = (sem, em.now(sem))

    for i in range(NT):
        em.dma("sync", x_tiles[i][:, :], y_in[128 * i:128 * (i + 1), :], ld)
    em.dma("sync", ident[:, :], ident_in, ld)
    ld_all = em.now(ld)
    em.wait("vector", ld, ld_all)
    em.wait("scalar", ld, ld_all)
    em.wait("tensor", ld, ld_all)

    for t in range(NT):
        em.wait("scalar", vP, em.now(vP))
        emit_ln(em, [x_tiles[t]], [xn], stats, None, [0], vP, sP, eps=1e-5)
        em.wait("scalar", vP, em.now(vP))
        emit_ln(em, [xn], [xn2], stats[:, 8:], None, [0], vP, sP, rms=True,
                eps=1e-6)
        em.op("vector", lambda h: h.drain())
        em.op("vector", lambda h: h.tensor_reduce(
            out=stats[:, 16:17], in_=xn2[:, :], axis=AX.X, op=ALU.max,
            apply_absolute_value=True))
        em.op("vector", lambda h: h.drain())
        em.op("vector", lambda h: h.tensor_scalar(
            out=stats[:, 16:17], in0=stats[:, 16:17], scalar1=1e-5,
            scalar2=1.0 / 127.0, op0=ALU.max, op1=ALU.mult))
        em.op("vector", lambda h: h.drain())
        em.op("vector", lambda h: h.reciprocal(stats[:, 17:18], stats[:, 16:17]))
        em.op("vector", lambda h: h.drain())
        for sl in range(4):
            em.op("vector", lambda h, sl=sl: h.tensor_scalar(
                out=zb[:, :], in0=xn2[:, 512 * sl:512 * (sl + 1)],
                scalar1=stats[:, 17:18], scalar2=None, op0=ALU.mult))
            em.op("vector", lambda h: h.drain())
            em.op("vector", lambda h: h.tensor_copy(zi[:, :], zb[:, :]))
            em.op("vector", lambda h: h.drain())
            em.op("vector", lambda h, sl=sl: h.tensor_scalar(
                out=xn2[:, 512 * sl:512 * (sl + 1)], in0=zi[:, :],
                scalar1=stats[:, 16:17], scalar2=None, op0=ALU.mult),
                inc=(vP if sl == 3 else None))
            em.op("vector", lambda h: h.drain())
        em.wait("tensor", vP, em.now(vP))
        for k in range(16):
            pe_write("p_tr")
            em.op("tensor", lambda h, k=k: h.transpose(
                p_tr[:, 0:128], xn2[:, 128 * k:128 * (k + 1)], ident[:, :]),
                inc=tP)
            em.wait("vector", tP, em.now(tP))
            em.op("vector", lambda h, k=k, t=t: h.tensor_copy(
                xq_fm[:, 1024 * k + 128 * t:1024 * k + 128 * (t + 1)],
                p_tr[:, 0:128]), inc=vP)
            consumed("p_tr", vP)
    prep_done = em.now(vP)
    em.wait("tensor", vP, prep_done)

    last_wd_tok = {0: None, 1: None}
    for tg in range(2):
        tiles = list(range(4 * tg, 4 * tg + 4))
        for n in range(16):
            em.wait("sync", tP, em.now(tP))
            for k in range(16):
                em.dma("sync", wg_sb[:, 512 * k:512 * (k + 1)],
                       qg_in[128 * k:128 * (k + 1), 512 * n:512 * (n + 1)], ld)
                em.dma("sync", wu_sb[:, 512 * k:512 * (k + 1)],
                       qu_in[128 * k:128 * (k + 1), 512 * n:512 * (n + 1)], ld)
            em.wait("tensor", ld, em.now(ld))
            for ti, t in enumerate(tiles):
                pe_write("p_g")
                pe_write("p_u")
                for k in range(16):
                    em.op("tensor", lambda h, k=k, t=t: h.matmul(
                        p_g[:, :], xq_fm[:, 1024 * k + 128 * t:1024 * k + 128 * (t + 1)],
                        wg_sb[:, 512 * k:512 * (k + 1)],
                        start=(k == 0), stop=(k == 15)))
                for k in range(16):
                    em.op("tensor", lambda h, k=k, t=t: h.matmul(
                        p_u[:, :], xq_fm[:, 1024 * k + 128 * t:1024 * k + 128 * (t + 1)],
                        wu_sb[:, 512 * k:512 * (k + 1)],
                        start=(k == 0), stop=(k == 15)),
                        inc=(tP if k == 15 else None))
                em.wait("scalar", tP, em.now(tP))
                em.wait("scalar", vP, em.now(vP))
                em.op("scalar", lambda h: h.activation(
                    out=gact[:, :], in_=p_g[:, :], func=AF.Silu), inc=sP)
                consumed("p_g", sP)
                em.wait("vector", sP, em.now(sP))
                em.op("vector", lambda h, t=t, ti=ti, n=n: h.tensor_tensor(
                    out=h_tg[ti][:, 512 * n:512 * (n + 1)], in0=p_u[:, :],
                    in1=gact[:, :], op=ALU.mult), inc=vP)
                consumed("p_u", vP)
        for ti, t in enumerate(tiles):
            em.wait("scalar", vP, em.now(vP))
            for sl in range(16):
                em.op("scalar", lambda h, ti=ti, sl=sl: h.activation(
                    out=gact[:, :], in_=h_tg[ti][:, 512 * sl:512 * (sl + 1)],
                    func=AF.Square, accum_out=acc16[:, sl:sl + 1]),
                    inc=(sP if sl == 15 else None))
                em.op("scalar", lambda h: h.drain())
            em.wait("vector", sP, em.now(sP))
            em.op("vector", lambda h: h.tensor_reduce(
                out=stats[:, 20:21], in_=acc16[:, :], axis=AX.X, op=ALU.add))
            em.op("vector", lambda h: h.drain())
            em.op("vector", lambda h: h.tensor_scalar(
                out=stats[:, 21:22], in0=stats[:, 20:21], scalar1=1.0 / 8192.0,
                scalar2=1e-6, op0=ALU.mult, op1=ALU.add))
            em.op("vector", lambda h: h.drain())
            em.op("vector", lambda h: h.reciprocal(stats[:, 22:23], stats[:, 21:22]),
                  inc=vP)
            em.wait("scalar", vP, em.now(vP))
            em.op("scalar", lambda h: h.activation(
                out=stats[:, 22:23], in_=stats[:, 22:23], func=AF.Sqrt), inc=sP)
            em.wait("vector", sP, em.now(sP))
            em.op("vector", lambda h: h.drain())
            em.op("vector", lambda h, ti=ti: h.tensor_reduce(
                out=stats[:, 23:24], in_=h_tg[ti][:, :], axis=AX.X, op=ALU.max,
                apply_absolute_value=True))
            em.op("vector", lambda h: h.drain())
            em.op("vector", lambda h: h.tensor_tensor(
                out=stats[:, 24:25], in0=stats[:, 23:24], in1=stats[:, 22:23],
                op=ALU.mult))
            em.op("vector", lambda h: h.drain())
            em.op("vector", lambda h: h.tensor_scalar(
                out=stats[:, 24:25], in0=stats[:, 24:25], scalar1=1e-5,
                scalar2=None, op0=ALU.max))
            em.op("vector", lambda h: h.drain())
            em.op("vector", lambda h: h.tensor_scalar(
                out=stats[:, 25:26], in0=stats[:, 24:25], scalar1=1.0 / 127.0,
                scalar2=None, op0=ALU.mult))
            em.op("vector", lambda h: h.drain())
            em.op("vector", lambda h: h.reciprocal(stats[:, 26:27], stats[:, 25:26]))
            em.op("vector", lambda h: h.drain())
            em.op("vector", lambda h: h.tensor_tensor(
                out=stats[:, 27:28], in0=stats[:, 26:27], in1=stats[:, 22:23],
                op=ALU.mult))
            em.op("vector", lambda h: h.drain())
            for sl in range(16):
                em.op("vector", lambda h, ti=ti, sl=sl: h.tensor_scalar(
                    out=zb[:, :], in0=h_tg[ti][:, 512 * sl:512 * (sl + 1)],
                    scalar1=stats[:, 27:28], scalar2=None, op0=ALU.mult))
                em.op("vector", lambda h: h.drain())
                em.op("vector", lambda h: h.tensor_copy(zi[:, :], zb[:, :]))
                em.op("vector", lambda h: h.drain())
                em.op("vector", lambda h, ti=ti, sl=sl: h.tensor_scalar(
                    out=h_tg[ti][:, 512 * sl:512 * (sl + 1)], in0=zi[:, :],
                    scalar1=stats[:, 25:26], scalar2=None, op0=ALU.mult),
                    inc=(vP if sl == 15 else None))
                em.op("vector", lambda h: h.drain())
            em.wait("tensor", vP, em.now(vP))
            for k3 in range(64):
                pe_write("p_tr")
                em.op("tensor", lambda h, ti=ti, k3=k3: h.transpose(
                    p_tr[:, 0:128], h_tg[ti][:, 128 * k3:128 * (k3 + 1)],
                    ident[:, :]), inc=tP)
                em.wait("vector", tP, em.now(tP))
                em.op("vector", lambda h, ti=ti, k3=k3: h.tensor_copy(
                    h_tg[ti][:, 128 * k3:128 * (k3 + 1)], p_tr[:, 0:128]), inc=vP)
                consumed("p_tr", vP)
        tr3_done = em.now(vP)
        em.wait("tensor", vP, tr3_done)
        for n in range(4):
            for ti in range(4):
                pe_write(f"p_y{ti}")
            for k3 in range(64):
                b = k3 % 2
                if last_wd_tok[b] is not None:
                    em.wait("sync", tP, last_wd_tok[b])
                em.dma("sync", wdb[b][:, :],
                       qd_in[128 * k3:128 * (k3 + 1), 512 * n:512 * (n + 1)], lw[b])
                em.wait("tensor", lw[b], em.now(lw[b]))
                for ti in range(4):
                    em.op("tensor", lambda h, ti=ti, k3=k3, b=b: h.matmul(
                        p_y[ti][:, :], h_tg[ti][:, 128 * k3:128 * (k3 + 1)],
                        wdb[b][:, :], start=(k3 == 0), stop=(k3 == 63)),
                        inc=(tP if ti == 3 else None))
                last_wd_tok[b] = em.now(tP)
            em.wait("vector", tP, em.now(tP))
            for ti, t in enumerate(tiles):
                em.op("vector", lambda h, t=t, ti=ti, n=n: h.tensor_tensor(
                    out=x_tiles[t][:, 512 * n:512 * (n + 1)],
                    in0=x_tiles[t][:, 512 * n:512 * (n + 1)],
                    in1=p_y[ti][:, :], op=ALU.add), inc=(vP if ti == 3 else None))
                consumed(f"p_y{ti}", vP)
            em.op("vector", lambda h: h.drain())
    done = em.now(vP)
    em.wait("sync", vP, done)
    for t in range(NT):
        em.dma("sync", y_out[128 * t:128 * (t + 1), :], x_tiles[t][:, :], st)
    em.wait("sync", st, em.now(st))
    em.emit_and_close()
    es.close()
    return nc


def run_launchE3(y3_full, qgT, quT, qdT):
    shards = shard_rows(np.asarray(y3_full, np.float32))
    nc = build_launchE3()
    in_maps = [{"y3": s, "qgT": qgT, "quT": quT, "qdT": qdT,
                "ident_hbm": IDENT_BF16} for s in shards]
    res = run_bass_kernel_spmd(nc, in_maps, core_ids=list(range(NCORES)))
    return unshard_rows([np.asarray(r["y4"]) for r in res.results])
